# version 04
# baseline (speedup 1.0000x reference)
import numpy as np
import ml_dtypes
from contextlib import ExitStack

import concourse.bass as bass
import concourse.mybir as mybir
from concourse.bass_utils import run_bass_kernel_spmd

F32 = mybir.dt.float32
BF16 = mybir.dt.bfloat16
AF = mybir.ActivationFunctionType
ALU = mybir.AluOpType
AX = mybir.AxisListType

NB = 2
T = 2048
DM = 1024
NCC = DM // 128

ALL_STAGES = ("p0", "p1", "p2", "p3", "p4")
DEV_STAGES = ALL_STAGES
DEV_TAPS = False

ENGS = ["pe", "act", "dve", "pool", "sp"]
DMA_ENGS = ["sp", "pool", "act"]


def _tname(ap):
    t = getattr(ap, "tensor", None)
    if t is None:
        return None
    return t.name if hasattr(t, "name") else str(t)


class G:

    def __init__(self, nc, es):
        self.nc = nc
        self.chain = {e: es.enter_context(nc.semaphore(f"chain_{e}"))
                      for e in ENGS}
        self.cnt = {e: 0 for e in ENGS}
        self.dma_sem = {e: es.enter_context(nc.semaphore(f"dma_{e}"))
                        for e in DMA_ENGS}
        self.dma_cnt = {e: 0 for e in DMA_ENGS}
        self.waited = {e: {} for e in ENGS}
        self.state = {}
        self.q = {"pe": nc.tensor, "act": nc.scalar, "dve": nc.vector,
                  "pool": nc.gpsimd, "sp": nc.sync}

    def _sem(self, semname):
        kind, e = semname.split("_", 1)
        return self.chain[e] if kind == "chain" else self.dma_sem[e]

    def _wait(self, eng, semname, val):
        if val <= 0:
            return
        w = self.waited[eng]
        if w.get(semname, 0) >= val:
            return
        self.q[eng].wait_ge(self._sem(semname), val)
        w[semname] = val

    def pre(self, eng, in_aps, out_aps):
        need = {}
        for ap in in_aps:
            st = self.state.get(_tname(ap))
            if st:
                for s, v in st["w"].items():
                    need[s] = max(need.get(s, 0), v)
        for ap in out_aps:
            st = self.state.get(_tname(ap))
            if st:
                for s, v in st["w"].items():
                    need[s] = max(need.get(s, 0), v)
                for s, v in st["r"].items():
                    need[s] = max(need.get(s, 0), v)
        for s, v in sorted(need.items()):
            self._wait(eng, s, v)

    def post(self, eng, ins, in_aps, out_aps, dma=False):
        if dma:
            self.dma_cnt[eng] += 16
            val = self.dma_cnt[eng]
            ins.then_inc(self.dma_sem[eng], 16)
            semname = f"dma_{eng}"
        else:
            self.cnt[eng] += 1
            val = self.cnt[eng]
            ins.then_inc(self.chain[eng], 1)
            semname = f"chain_{eng}"
        for ap in out_aps:
            n = _tname(ap)
            if n is not None:
                st = self.state.setdefault(n, {"w": {}, "r": {}})
                st["w"][semname] = max(st["w"].get(semname, 0), val)
        for ap in in_aps:
            n = _tname(ap)
            if n is not None:
                st = self.state.setdefault(n, {"w": {}, "r": {}})
                st["r"][semname] = max(st["r"].get(semname, 0), val)
        return ins

    def op(self, eng, fname, o_, i_, **kw):
        outs = o_ if isinstance(o_, list) else [o_]
        self.pre(eng, i_, outs)
        ins_obj = getattr(self.q[eng], fname)(**kw)
        self.post(eng, ins_obj, i_, outs)
        return ins_obj

    def dma(self, eng, out, in_):
        self.pre(eng, [in_], [out])
        ins_obj = self.q[eng].dma_start(out=out, in_=in_)
        self.post(eng, ins_obj, [in_], [out], dma=True)
        return ins_obj

    def mm(self, psum_out, pairs, extra_ins=()):
        in_aps = [ap for p in pairs for ap in p] + list(extra_ins)
        self.pre("pe", in_aps, [psum_out])
        n = len(pairs)
        ins_obj = None
        for i, (l, r) in enumerate(pairs):
            ins_obj = self.q["pe"].matmul(psum_out, l, r, start=(i == 0),
                                          stop=(i == n - 1))
        self.post("pe", ins_obj, in_aps, [psum_out])
        return ins_obj

    def mm_multi(self, items, outs, ins):
        self.pre("pe", ins, outs)
        ins_obj = None
        for ps, l, r, st, sp in items:
            ins_obj = self.q["pe"].matmul(ps, l, r, start=st, stop=sp)
        self.post("pe", ins_obj, ins, outs)
        return ins_obj

    def tr(self, psum_out, in_ap, identity):
        self.pre("pe", [in_ap, identity], [psum_out])
        ins_obj = self.q["pe"].transpose(out=psum_out, in_=in_ap,
                                         identity=identity)
        self.post("pe", ins_obj, [in_ap, identity], [psum_out])
        return ins_obj

    def barrier(self):
        for e in ENGS:
            for p in ENGS:
                if p != e:
                    self._wait(e, f"chain_{p}", self.cnt[p])
            for p in DMA_ENGS:
                self._wait(e, f"dma_{p}", self.dma_cnt[p])


def host_constants():
    c = {}
    c["c_eyeb"] = np.eye(128, dtype=ml_dtypes.bfloat16)
    c["c_eyef"] = np.eye(128, dtype=np.float32)
    m = np.zeros((128, 128), np.float32)
    m[np.triu_indices(128, 1)] = -1e38
    c["c_mask"] = m
    half = 32
    inv_freq = 1.0 / 10000.0 ** (np.arange(half, dtype=np.float64) / half)
    t = np.arange(T, dtype=np.float64)
    freqs = np.outer(t, inv_freq)
    emb = np.concatenate([freqs, freqs], 1)
    cos = np.cos(emb).T
    sin = np.sin(emb).T
    cos2 = np.concatenate([cos, cos], 0)
    sin2 = np.concatenate([sin, sin], 0)
    c["c_cos"] = cos2.astype(ml_dtypes.bfloat16)
    c["c_sin"] = sin2.astype(np.float32)
    c["c_cosq"] = (cos2 * 0.125).astype(ml_dtypes.bfloat16)
    c["c_sinq"] = (sin2 * 0.125).astype(np.float32)
    P = np.zeros((128, 128), np.float64)
    for i in range(128):
        base = (i // 64) * 64
        j = i % 64
        if j < 32:
            P[i, base + j + 32] = -1.0
        else:
            P[i, base + j - 32] = 1.0
    c["c_perm"] = P.T.astype(ml_dtypes.bfloat16)
    oh16 = np.zeros((16, 16 * 128), np.float32)
    for e in range(16):
        oh16[e, e * 128:(e + 1) * 128] = 1.0
    c["c_oh16"] = oh16.astype(ml_dtypes.bfloat16)
    mt4 = np.zeros((128, 4, 512), np.float32)
    for m in range(4):
        for i in range(128):
            mt4[i, m, :min(512, m * 128 + i)] = -1e38
    c["c_maskT4"] = mt4.reshape(128, 2048)
    oh7 = np.zeros((7, 7 * 128), np.float32)
    for e in range(7):
        oh7[e, e * 128:(e + 1) * 128] = 1.0
    c["c_oh7"] = oh7.astype(ml_dtypes.bfloat16)
    return c


def _load_cols(g, dst, src_ap, n, eng="act"):
    v = src_ap.rearrange("(n p) -> n p", p=128)
    for i in range(n):
        g.dma(eng, dst[:, i:i + 1], v[i])


def build_layernorm(g, bufs, s, psS, psQ, psB1, psB2, emit_out):
    nc = g.nc
    onescol, ones_k1, epscol = bufs["onescol"], bufs["ones_k1"], bufs["epscol"]
    sq = bufs["sq"]
    items = []
    for cc in range(NCC):
        g.op("act", "activation", sq[:],
             [s[:, cc * 1024:(cc + 1) * 1024]],
             out=sq[:],
             in_=s[:, cc * 1024:(cc + 1) * 1024], func=AF.Square)
        for th in range(2):
            g.mm_multi(
                [(psS[0:1, th * 512:(th + 1) * 512], onescol[:],
                  s[:, cc * 1024 + th * 512: cc * 1024 + (th + 1) * 512],
                  cc == 0, cc == NCC - 1)],
                outs=[psS[0:1, :]], ins=[onescol[:], s[:]])
            g.mm_multi(
                [(psQ[0:1, th * 512:(th + 1) * 512], onescol[:],
                  sq[:, th * 512:th * 512 + 512],
                  cc == 0, cc == NCC - 1)],
                outs=[psQ[0:1, :]], ins=[onescol[:], sq[:]])
    meanr, m2r, varr = bufs["meanr"], bufs["m2r"], bufs["varr"]
    g.op("dve", "tensor_scalar", meanr[:], [psS[0:1, :]], out=meanr[:],
         in0=psS[0:1, :], scalar1=1.0 / DM, scalar2=None, op0=ALU.mult)
    g.op("dve", "tensor_scalar", m2r[:], [psQ[0:1, :]], out=m2r[:],
         in0=psQ[0:1, :], scalar1=1.0 / DM, scalar2=None, op0=ALU.mult)
    g.op("act", "activation", varr[:], [meanr[:]], out=varr[:], in_=meanr[:],
         func=AF.Square)
    g.op("dve", "tensor_sub", varr[:], [m2r[:], varr[:]], out=varr[:],
         in0=m2r[:], in1=varr[:])
    g.op("act", "activation", varr[:], [varr[:], epscol[:]], out=varr[:],
         in_=varr[:], func=AF.Sqrt, bias=epscol[:])
    g.op("dve", "reciprocal", varr[:], [varr[:]], out=varr[:], in_=varr[:])
    for th in range(2):
        g.mm(psB1[:, th * 512:(th + 1) * 512],
             [(ones_k1[:], meanr[0:1, th * 512:(th + 1) * 512])])
        g.mm(psB2[:, th * 512:(th + 1) * 512],
             [(ones_k1[:], varr[0:1, th * 512:(th + 1) * 512])])
    lnf = bufs["lnf"]
    for cc in range(NCC):
        sl = lnf[:]
        g.op("dve", "tensor_sub", sl, [s[:, cc * 1024:(cc + 1) * 1024],
                                       psB1[:]],
             out=sl, in0=s[:, cc * 1024:(cc + 1) * 1024], in1=psB1[:])
        g.op("dve", "tensor_mul", sl, [sl, psB2[:]], out=sl, in0=sl,
             in1=psB2[:])
        emit_out(cc, sl)


def build_program(stages=ALL_STAGES, dev_taps=False):
    nc = bass.Bass("TRN2", target_bir_lowering=False)
    cst = host_constants()

    dram = {}

    def dt_(name, shape, dtype, kind):
        dram[name] = nc.dram_tensor(name, list(shape), dtype, kind=kind).ap()
        return dram[name]

    first = stages[0]
    stream_reader = {"p1": "sA", "p2": "sB", "p3": "sA", "p4": "sB"}
    stream_reader_b = {"p2": "sBb", "p3": "sAb", "p4": "sBb"}
    written = set()
    for s_ in stages:
        if s_ == "p0":
            written.add("sA")
        elif s_ == "p1":
            written.update(["sB", "sBb"])
        elif s_ == "p2":
            written.update(["sA", "sAb"])
        elif s_ == "p3":
            written.update(["sB", "sBb"])
    needed_in = set()
    if first != "p0":
        needed_in.add(stream_reader[first])
        if first in stream_reader_b:
            needed_in.add(stream_reader_b[first])
    needed_in -= written

    def stream_kind(name):
        if name in needed_in:
            return "ExternalInput"
        if dev_taps and name in written:
            return "ExternalOutput"
        return "Internal"

    if "p0" in stages:
        dt_("xin", (NB, T, DM), F32, "ExternalInput")
    for name in ["sA", "sB"]:
        if name in written or name in needed_in:
            dt_(name, (NB, DM, T), F32, stream_kind(name))
    for name in ["sAb", "sBb"]:
        if name in written or name in needed_in:
            dt_(name, (NB, DM, T), BF16, stream_kind(name))
    if "p4" in stages:
        dt_("y", (NB, T, DM), F32, "ExternalOutput")

    win = {}

    def wt(name, shape):
        win[name] = nc.dram_tensor(name, list(shape), F32,
                                   kind="ExternalInput").ap()
        return win[name]

    if "p1" in stages:
        for n_ in ["L0_attn_w", "L0_attn_u"]:
            wt(n_, (DM,))
        for n_ in ["L0_attn_time_mix_k", "L0_attn_time_mix_v",
                   "L0_attn_time_mix_r"]:
            wt(n_, (DM,))
        for n_ in ["L0_attn_Wk", "L0_attn_Wv", "L0_attn_Wr", "L0_attn_Wo"]:
            wt(n_, (DM, DM))
    if "p2" in stages:
        wt("L0_ffn_Wm", (DM, DM))
        wt("L0_ffn_bm", (DM,))
        wt("L0_ffn_expert_emb", (128, 16))
        wt("L0_ffn_W1", (16, 128, 512))
        wt("L0_ffn_b1", (16, 512))
        wt("L0_ffn_W2", (16, 512, 128))
        wt("L0_ffn_b2", (16, 128))
        wt("L0_ffn_Wo", (DM, DM))
        wt("L0_ffn_bo", (DM,))
    if "p3" in stages:
        wt("L1_attn_Wq", (DM, DM))
        wt("L1_attn_Wk", (DM, 256))
        wt("L1_attn_Wv", (DM, 256))
        wt("L1_attn_Wo", (DM, DM))
    if "p4" in stages:
        wt("L1_ffn_router_w", (7, DM))
        wt("L1_ffn_w_gate", (7, DM, 4096))
        wt("L1_ffn_w_up", (7, DM, 4096))
        wt("L1_ffn_w_down", (7, 4096, DM))
        wt("L1_ffn_sw_gate", (1, DM, 4096))
        wt("L1_ffn_sw_up", (1, DM, 4096))
        wt("L1_ffn_sw_down", (1, 4096, DM))
        dt_("gate_s", (7, 8, 128, 4096), BF16, "Internal")
        dt_("up_s", (7, 8, 128, 4096), BF16, "Internal")
        dt_("down_s", (7, 2, 4, 128, 4096), BF16, "Internal")
        dt_("sg_s", (8, 128, 4096), BF16, "Internal")
        dt_("su_s", (8, 128, 4096), BF16, "Internal")
        dt_("sd_s", (2, 4, 128, 4096), BF16, "Internal")

    cin = {}
    cshape = {"c_eyeb": (128, 128), "c_eyef": (128, 128),
              "c_mask": (128, 128), "c_maskT4": (128, 2048),
              "c_cos": (128, T), "c_sin": (128, T),
              "c_cosq": (128, T), "c_sinq": (128, T), "c_perm": (128, 128),
              "c_oh16": (16, 2048), "c_oh7": (7, 896)}
    cdt = {"c_eyeb": BF16, "c_eyef": F32, "c_mask": F32, "c_maskT4": F32,
           "c_cos": BF16,
           "c_sin": F32, "c_cosq": BF16, "c_sinq": F32, "c_perm": BF16,
           "c_oh16": BF16, "c_oh7": BF16}
    for n_ in cshape:
        cin[n_] = nc.dram_tensor(n_, list(cshape[n_]), cdt[n_],
                                 kind="ExternalInput").ap()

    es = ExitStack()
    with es:
        g = G(nc, es)
        ctr = [0]

        def sb(st, name, shape, dtype):
            ctr[0] += 1
            return st.enter_context(
                nc.sbuf_tensor(f"{name}_{ctr[0]}", list(shape), dtype))

        def psum(st, name, shape, dtype=F32):
            ctr[0] += 1
            return st.enter_context(
                nc.psum_tensor(f"{name}_{ctr[0]}", list(shape), dtype))

        dummy = sb(es, "dummy_w", (128, 8), F32)
        eyeb = sb(es, "eyeb", (128, 128), BF16)
        eyef = sb(es, "eyef", (128, 128), F32)
        onescol = sb(es, "onescol", (128, 1), F32)
        ones_k1 = sb(es, "ones_k1", (1, 128), F32)
        epscol = sb(es, "epscol", (1, 1), F32)
        convb = sb(es, "convb", (128, 2 * 4096), BF16)

        gl = {"onescol": onescol[:], "ones_k1": ones_k1[:],
              "epscol": epscol[:]}

        block = es.enter_context(nc.Block())

        @block.sync
        def _(_sync):
            _build_body(nc, g, es, sb, psum, dram, win, cin, stages, gl,
                        dummy, eyeb, eyef, convb)

    declared = set()
    if "p0" in stages:
        declared.add("xin")
    declared |= set(win.keys()) | set(cin.keys())
    declared |= needed_in
    return nc, cst, declared


def _build_body(nc, g, es, sb, psum, dram, win, cin, stages, gl, dummy,
                eyeb, eyef, convb):
    onescol = gl["onescol"]
    ones_k1 = gl["ones_k1"]
    epscol = gl["epscol"]

    g.op("act", "memzero", dummy[:], [], ap=dummy[:])
    for fn in [AF.Copy, AF.Exp, AF.Sigmoid, AF.Silu, AF.Gelu, AF.Sign,
               AF.Square, AF.Sqrt]:
        g.op("act", "activation", dummy[:], [dummy[:]], out=dummy[:],
             in_=dummy[:], func=fn)
    g.op("dve", "memset", onescol[:], [], ap=onescol[:], constant=1.0)
    g.op("dve", "memset", ones_k1[:], [], ap=ones_k1[:], constant=1.0)
    g.op("dve", "memset", epscol[:], [], ap=epscol[:], constant=1e-5)
    g.dma("sp", eyeb[:], cin["c_eyeb"])
    g.dma("sp", eyef[:], cin["c_eyef"])

    if "p4" in stages:
        ci = [0]

        def conv_gu(src_ap, dst_ap):
            v = src_ap.rearrange("(cc p) h -> p cc h", p=128)
            for hs in range(8):
                buf = convb[:, (ci[0] % 2) * 4096:(ci[0] % 2) * 4096 + 4096]
                ci[0] += 1
                g.dma("pool", buf.rearrange("p (cc h) -> p cc h", cc=8),
                      v[:, :, hs * 512:(hs + 1) * 512])
                g.dma("pool", dst_ap[hs], buf)

        def conv_dn(src_ap, dst_ap):
            for p_ in range(2):
                for hr in range(4):
                    buf = convb[:, (ci[0] % 2) * 4096:
                                (ci[0] % 2) * 4096 + 4096]
                    ci[0] += 1
                    v = src_ap[hr * 1024:(hr + 1) * 1024,
                               p_ * 512:(p_ + 1) * 512].rearrange(
                        "(hcc p) d -> p hcc d", p=128)
                    g.dma("pool",
                          buf.rearrange("p (hcc d) -> p hcc d", hcc=8), v)
                    g.dma("pool", dst_ap[p_, hr], buf)

        for e in range(7):
            conv_gu(win["L1_ffn_w_gate"][e], dram["gate_s"][e])
            conv_gu(win["L1_ffn_w_up"][e], dram["up_s"][e])
            conv_dn(win["L1_ffn_w_down"][e], dram["down_s"][e])
        conv_gu(win["L1_ffn_sw_gate"][0], dram["sg_s"])
        conv_gu(win["L1_ffn_sw_up"][0], dram["su_s"])
        conv_dn(win["L1_ffn_sw_down"][0], dram["sd_s"])

    if "p0" in stages:
        _stage_p0(nc, g, es, sb, psum, dram, eyef)
    if "p1" in stages:
        _stage_p1(nc, g, es, sb, psum, dram, win, gl, eyeb)
    if "p2" in stages:
        _stage_p2(nc, g, es, sb, psum, dram, win, cin, gl, eyeb, eyef)
    if "p3" in stages:
        _stage_p3(nc, g, es, sb, psum, dram, win, cin, gl, eyeb)
    if "p4" in stages:
        _stage_p4(nc, g, es, sb, psum, dram, win, cin, gl, eyeb, eyef)
    g.barrier()


from contextlib import contextmanager


@contextmanager
def _scoped(g):
    st = ExitStack()
    try:
        with st:
            yield st
            g.barrier()
    finally:
        pass


def _stage_p0(nc, g, es, sb, psum, dram, eyef):
    xin, sA = dram["xin"], dram["sA"]
    with _scoped(g) as st:
        xt = [sb(st, f"p0x{i}", (128, DM), F32) for i in range(4)]
        asm = [sb(st, f"p0a{c}", (128, 512), F32) for c in range(NCC)]
        psT = [psum(st, f"p0p{i}", (128, 512), F32) for i in range(2)]
        for b in range(NB):
            for tg in range(4):
                t0 = tg * 512
                for ti in range(4):
                    g.dma("sp", xt[ti][:],
                          xin[b, t0 + ti * 128:t0 + (ti + 1) * 128, :])
                for cc in range(NCC):
                    ps = psT[cc % 2]
                    for ti in range(4):
                        g.tr(ps[:, ti * 128:(ti + 1) * 128],
                             xt[ti][:, cc * 128:(cc + 1) * 128], eyef[:])
                    g.op("dve", "tensor_copy", asm[cc][:], [ps[:]],
                         out=asm[cc][:], in_=ps[:])
                    g.dma("act", sA[b, cc * 128:(cc + 1) * 128, t0:t0 + 512],
                          asm[cc][:])
    g.barrier()


def _stage_p1(nc, g, es, sb, psum, dram, win, gl, eyeb):
    sA, sB, sBb = dram["sA"], dram["sB"], dram["sBb"]
    onescol, ones_k1, epscol = gl["onescol"], gl["ones_k1"], gl["epscol"]
    with _scoped(g) as st:
        wT = {}
        for nm in ["Wk", "Wv", "Wr", "Wo"]:
            wT[nm] = sb(st, "p1" + nm, (128, NCC * DM), BF16)
        with _scoped(g) as sw:
            wtmp = sb(sw, "p1wtmp", (128, DM), BF16)
            wtmpf = sb(sw, "p1wtmpf", (128, DM), F32)
            psWb = psum(sw, "p1psw", (128, 512), BF16)
            for nm in ["Wk", "Wv", "Wr", "Wo"]:
                W = win["L0_attn_" + nm]
                for dc in range(NCC):
                    g.dma("sp", wtmpf[:], W[dc * 128:(dc + 1) * 128, :])
                    g.op("dve", "tensor_copy", wtmp[:], [wtmpf[:]],
                         out=wtmp[:], in_=wtmpf[:])
                    for cc in range(NCC):
                        ps = psWb[:, (cc % 4) * 128:(cc % 4 + 1) * 128]
                        g.tr(ps, wtmp[:, cc * 128:(cc + 1) * 128], eyeb[:])
                        g.op("dve", "tensor_copy",
                             wT[nm][:, cc * DM + dc * 128:
                                    cc * DM + (dc + 1) * 128],
                             [psWb[:]],
                             out=wT[nm][:, cc * DM + dc * 128:
                                        cc * DM + (dc + 1) * 128],
                             in_=ps)
            g.barrier()
        ucol = sb(st, "p1u", (128, NCC), F32)
        edec = sb(st, "p1edec", (128, NCC), F32)
        tmcol = {}
        _load_cols(g, edec, win["L0_attn_w"], NCC)
        g.op("act", "activation", edec[:], [edec[:]], out=edec[:],
             in_=edec[:], func=AF.Exp)
        g.op("dve", "tensor_scalar_mul", edec[:], [edec[:]], out=edec[:],
             in0=edec[:], scalar1=-1.0)
        g.op("act", "activation", edec[:], [edec[:]], out=edec[:],
             in_=edec[:], func=AF.Exp)
        _load_cols(g, ucol, win["L0_attn_u"], NCC)
        for nm in ["k", "v", "r"]:
            tmcol[nm] = sb(st, "p1tm" + nm, (128, NCC), F32)
            _load_cols(g, tmcol[nm], win["L0_attn_time_mix_" + nm], NCC)

        ones1024 = sb(st, "p1ones", (128, 1024), F32)
        g.op("dve", "memset", ones1024[:], [], ap=ones1024[:], constant=1.0)
        carryA = sb(st, "p1cA", (128, NCC), F32)
        carryB = sb(st, "p1cB", (128, NCC), F32)
        negMs = sb(st, "p1nM", (128, NCC), F32)
        colt = sb(st, "p1colt", (128, 1), F32)

        psK = psum(st, "p1psK", (128, 1024), F32)
        psV = psum(st, "p1psV", (128, 1024), F32)
        psR = psum(st, "p1psR", (128, 1024), F32)
        psO = psum(st, "p1psO", (128, 1024), F32)

        for b in range(NB):
            g.op("dve", "memset", carryA[:], [], ap=carryA[:], constant=0.0)
            g.op("dve", "memset", carryB[:], [], ap=carryB[:], constant=0.0)
            for hf in range(2):
                t0 = hf * 1024
                with _scoped(g) as sh:
                    mixK = sb(sh, "p1mK", (128, NCC * 1024), BF16)
                    mixV = sb(sh, "p1mV", (128, NCC * 1024), BF16)
                    mixR = sb(sh, "p1mR", (128, NCC * 1024), BF16)
                    rwkvb = sb(sh, "p1rw", (128, NCC * 1024), BF16)
                    with _scoped(g) as sm:
                        xc = sb(sm, "p1xc", (128, 1025), F32)
                        tdiff = sb(sm, "p1td", (128, 1024), F32)
                        for cc in range(NCC):
                            if hf == 0:
                                g.op("dve", "memset", xc[:, 0:1], [],
                                     ap=xc[:, 0:1], constant=0.0)
                                g.dma("sp", xc[:, 1:1025],
                                      sA[b, cc * 128:(cc + 1) * 128,
                                         t0:t0 + 1024])
                            else:
                                g.dma("sp", xc[:],
                                      sA[b, cc * 128:(cc + 1) * 128,
                                         t0 - 1:t0 + 1024])
                            g.op("dve", "tensor_sub", tdiff[:],
                                 [xc[:]], out=tdiff[:], in0=xc[:, 1:1025],
                                 in1=xc[:, 0:1024])
                            for nm, mx in [("k", mixK), ("v", mixV),
                                           ("r", mixR)]:
                                sl = mx[:, cc * 1024:(cc + 1) * 1024]
                                g.op("dve", "scalar_tensor_tensor", sl,
                                     [tdiff[:], tmcol[nm][:], xc[:]],
                                     out=sl, in0=tdiff[:],
                                     scalar=tmcol[nm][:, cc:cc + 1],
                                     in1=xc[:, 0:1024], op0=ALU.mult,
                                     op1=ALU.add)
                    with _scoped(g) as sm:
                        kg = sb(sm, "p1kg", (128, 1024), F32)
                        vg = sb(sm, "p1vg", (128, 1024), F32)
                        sigr = sb(sm, "p1sr", (128, 1024), F32)
                        zk = sb(sm, "p1zk", (128, 1024), F32)
                        zkv = sb(sm, "p1zkv", (128, 1024), F32)
                        decb = sb(sm, "p1dec", (128, 1024), F32)
                        aS = sb(sm, "p1aS", (128, 1024), F32)
                        bS = sb(sm, "p1bS", (128, 1024), F32)
                        den = sb(sm, "p1den", (128, 1024), F32)
                        num = sb(sm, "p1num", (128, 1024), F32)
                        negM = sb(sm, "p1negM", (128, 1), F32)
                        ubias = sb(sm, "p1ub", (128, 1), F32)
                        for gc in range(NCC):
                            for ps_, mx, dst in [(psK, mixK, kg),
                                                 (psV, mixV, vg)]:
                                for th in range(2):
                                    g.mm(ps_[:, th * 512:(th + 1) * 512],
                                         [(wT["Wk" if dst is kg else "Wv"]
                                           [:, cc * DM + gc * 128:
                                            cc * DM + (gc + 1) * 128],
                                           mx[:, cc * 1024 + th * 512:
                                              cc * 1024 + th * 512 + 512])
                                          for cc in range(NCC)])
                                g.op("dve", "tensor_copy", dst[:], [ps_[:]],
                                     out=dst[:], in_=ps_[:])
                            for th in range(2):
                                g.mm(psR[:, th * 512:(th + 1) * 512],
                                     [(wT["Wr"][:, cc * DM + gc * 128:
                                                cc * DM + (gc + 1) * 128],
                                       mixR[:, cc * 1024 + th * 512:
                                            cc * 1024 + th * 512 + 512])
                                      for cc in range(NCC)])
                            g.op("act", "activation", sigr[:], [psR[:]],
                                 out=sigr[:], in_=psR[:], func=AF.Sigmoid)
                            g.op("dve", "reduce_max", negM[:], [kg[:]],
                                 out=negM[:], in_=kg[:], axis=AX.X,
                                 negate=True)
                            if hf == 1:
                                g.op("dve", "tensor_sub", colt[:],
                                     [negM[:], negMs[:]], out=colt[:],
                                     in0=negM[:],
                                     in1=negMs[:, gc:gc + 1])
                                g.op("act", "activation", colt[:], [colt[:]],
                                     out=colt[:], in_=colt[:], func=AF.Exp)
                                g.op("dve", "tensor_mul",
                                     carryA[:, gc:gc + 1],
                                     [carryA[:], colt[:]],
                                     out=carryA[:, gc:gc + 1],
                                     in0=carryA[:, gc:gc + 1], in1=colt[:])
                                g.op("dve", "tensor_mul",
                                     carryB[:, gc:gc + 1],
                                     [carryB[:], colt[:]],
                                     out=carryB[:, gc:gc + 1],
                                     in0=carryB[:, gc:gc + 1], in1=colt[:])
                            else:
                                g.op("dve", "tensor_copy",
                                     negMs[:, gc:gc + 1], [negM[:]],
                                     out=negMs[:, gc:gc + 1], in_=negM[:])
                            g.op("act", "activation", zk[:], [kg[:], negM[:]],
                                 out=zk[:], in_=kg[:], func=AF.Exp,
                                 bias=negM[:])
                            g.op("dve", "tensor_mul", zkv[:], [zk[:], vg[:]],
                                 out=zkv[:], in0=zk[:], in1=vg[:])
                            g.op("dve", "tensor_scalar_mul", decb[:],
                                 [ones1024[:], edec[:]], out=decb[:],
                                 in0=ones1024[:],
                                 scalar1=edec[:, gc:gc + 1])
                            g.op("dve", "tensor_tensor_scan", aS[:],
                                 [decb[:], zkv[:], carryA[:]], out=aS[:],
                                 data0=decb[:], data1=zkv[:],
                                 initial=carryA[:, gc:gc + 1],
                                 op0=ALU.mult, op1=ALU.add)
                            g.op("dve", "tensor_tensor_scan", bS[:],
                                 [decb[:], zk[:], carryB[:]], out=bS[:],
                                 data0=decb[:], data1=zk[:],
                                 initial=carryB[:, gc:gc + 1],
                                 op0=ALU.mult, op1=ALU.add)
                            g.op("dve", "tensor_add", ubias[:],
                                 [ucol[:], negM[:]], out=ubias[:],
                                 in0=ucol[:, gc:gc + 1], in1=negM[:])
                            g.op("act", "activation", zk[:],
                                 [kg[:], ubias[:]], out=zk[:], in_=kg[:],
                                 func=AF.Exp, bias=ubias[:])
                            g.op("dve", "tensor_mul", zkv[:], [zk[:], vg[:]],
                                 out=zkv[:], in0=zk[:], in1=vg[:])
                            g.op("dve", "tensor_add", den[:, 1:1024],
                                 [bS[:], zk[:]], out=den[:, 1:1024],
                                 in0=bS[:, 0:1023], in1=zk[:, 1:1024])
                            g.op("dve", "tensor_add", den[:, 0:1],
                                 [carryB[:], zk[:]], out=den[:, 0:1],
                                 in0=carryB[:, gc:gc + 1], in1=zk[:, 0:1])
                            g.op("dve", "tensor_scalar_max", den[:], [den[:]],
                                 out=den[:], in0=den[:], scalar1=1e-38)
                            g.op("dve", "reciprocal", den[:], [den[:]],
                                 out=den[:], in_=den[:])
                            g.op("dve", "tensor_add", num[:, 1:1024],
                                 [aS[:], zkv[:]], out=num[:, 1:1024],
                                 in0=aS[:, 0:1023], in1=zkv[:, 1:1024])
                            g.op("dve", "tensor_add", num[:, 0:1],
                                 [carryA[:], zkv[:]], out=num[:, 0:1],
                                 in0=carryA[:, gc:gc + 1], in1=zkv[:, 0:1])
                            g.op("dve", "tensor_copy", carryA[:, gc:gc + 1],
                                 [aS[:]], out=carryA[:, gc:gc + 1],
                                 in_=aS[:, 1023:1024])
                            g.op("dve", "tensor_copy", carryB[:, gc:gc + 1],
                                 [bS[:]], out=carryB[:, gc:gc + 1],
                                 in_=bS[:, 1023:1024])
                            g.op("dve", "tensor_mul", num[:], [num[:], den[:]],
                                 out=num[:], in0=num[:], in1=den[:])
                            g.op("dve", "tensor_mul",
                                 rwkvb[:, gc * 1024:(gc + 1) * 1024],
                                 [sigr[:], num[:]],
                                 out=rwkvb[:, gc * 1024:(gc + 1) * 1024],
                                 in0=sigr[:], in1=num[:])
                    with _scoped(g) as sm:
                        sbf = sb(sm, "p1s", (128, NCC * 1024), F32)
                        xres = sb(sm, "p1xres", (128, 1024), F32)
                        bufs = {"onescol": onescol, "ones_k1": ones_k1,
                                "epscol": epscol,
                                "sq": sb(sm, "p1sq", (128, 1024), F32),
                                "meanr": sb(sm, "p1mr", (1, 1024), F32),
                                "m2r": sb(sm, "p1m2", (1, 1024), F32),
                                "varr": sb(sm, "p1vr", (1, 1024), F32),
                                "lnf": sb(sm, "p1lnf", (128, 1024), F32)}
                        lnb = sb(sm, "p1lnb", (128, 1024), BF16)
                        for dc in range(NCC):
                            for th in range(2):
                                g.mm(psO[:, th * 512:(th + 1) * 512],
                                     [(wT["Wo"][:, cc * DM + dc * 128:
                                                cc * DM + (dc + 1) * 128],
                                       rwkvb[:, cc * 1024 + th * 512:
                                             cc * 1024 + th * 512 + 512])
                                      for cc in range(NCC)])
                            g.dma("sp", xres[:],
                                  sA[b, dc * 128:(dc + 1) * 128,
                                     t0:t0 + 1024])
                            g.op("dve", "tensor_add",
                                 sbf[:, dc * 1024:(dc + 1) * 1024],
                                 [psO[:], xres[:]],
                                 out=sbf[:, dc * 1024:(dc + 1) * 1024],
                                 in0=psO[:], in1=xres[:])

                        def emit(cc, sl):
                            bsl = lnb[:]
                            g.op("act", "copy", bsl, [sl], out=bsl, in_=sl)
                            g.dma("sp", sB[b, cc * 128:(cc + 1) * 128,
                                           t0:t0 + 1024], sl)
                            g.dma("act", sBb[b, cc * 128:(cc + 1) * 128,
                                             t0:t0 + 1024], bsl)

                        build_layernorm(g, bufs, sbf[:], psK, psV, psR, psO,
                                        emit)
    g.barrier()


def _stage_p2(nc, g, es, sb, psum, dram, win, cin, gl, eyeb, eyef):
    sB, sBb, sA, sAb = dram["sB"], dram["sBb"], dram["sA"], dram["sAb"]
    onescol, ones_k1, epscol = gl["onescol"], gl["ones_k1"], gl["epscol"]
    with _scoped(g) as st:
        wm = sb(st, "p2wm", (128, NCC * DM), BF16)
        wo = sb(st, "p2wo", (128, NCC * DM), BF16)
        w1 = sb(st, "p2w1", (128, 16 * 512), BF16)
        w2 = sb(st, "p2w2", (128, 16 * 4 * 128), BF16)
        embt = sb(st, "p2emb", (128, 16), BF16)
        b2t = sb(st, "p2b2", (16, 128), BF16)
        oh16 = sb(st, "p2oh", (16, 2048), BF16)
        bmcol = sb(st, "p2bm", (128, NCC), F32)
        bocol = sb(st, "p2bo", (128, NCC), F32)
        b1col = sb(st, "p2b1", (128, 64), F32)
        stg = sb(st, "p2stg", (128, 2048), F32)

        def lc2(dst, srcv, blocks, bw):
            gsz = max(1, 2048 // bw)
            for q in range(0, blocks, gsz):
                nb = min(gsz, blocks - q)
                w_ = nb * bw
                g.dma("sp",
                      stg[:, 0:w_].rearrange("p (n w) -> p n w", n=nb),
                      srcv[:, q:q + nb, :])
                g.op("dve", "tensor_copy", dst[:, q * bw:q * bw + w_],
                     [stg[:]], out=dst[:, q * bw:q * bw + w_],
                     in_=stg[:, 0:w_])

        lc2(wm[:], win["L0_ffn_Wm"].rearrange(
            "(cc p) d -> p cc d", p=128), NCC, DM)
        lc2(wo[:], win["L0_ffn_Wo"].rearrange(
            "(cc p) d -> p cc d", p=128), NCC, DM)
        lc2(w1[:], win["L0_ffn_W1"].rearrange("e d i -> d e i"), 16, 512)
        lc2(w2[:], win["L0_ffn_W2"].rearrange(
            "e (ic p) o -> p (e ic) o", p=128), 64, 128)
        g.dma("sp", stg[:, 0:16], win["L0_ffn_expert_emb"])
        g.op("dve", "tensor_copy", embt[:], [stg[:]], out=embt[:],
             in_=stg[:, 0:16])
        g.dma("sp", stg[0:16, 0:128], win["L0_ffn_b2"])
        g.op("dve", "tensor_copy", b2t[:], [stg[:]], out=b2t[:],
             in_=stg[0:16, 0:128])
        g.dma("sp", oh16[:], cin["c_oh16"])
        _load_cols(g, bmcol, win["L0_ffn_bm"], NCC)
        _load_cols(g, bocol, win["L0_ffn_bo"], NCC)
        b1v = win["L0_ffn_b1"].rearrange("e (ic p) -> e ic p", p=128)
        for e in range(16):
            for ic in range(4):
                g.dma("act", b1col[:, e * 4 + ic:e * 4 + ic + 1],
                      b1v[e, ic])

        for tt in range(4):
            b, t0 = tt // 2, (tt % 2) * 1024
            with _scoped(g) as s2:
                xb = sb(s2, "p2xb", (128, NCC * 1024), BF16)
                hb = sb(s2, "p2hb", (128, NCC * 1024), BF16)
                acc = sb(s2, "p2acc", (128, NCC * 1024), F32)
                cT = sb(s2, "p2cT", (16, 1024), BF16)
                lg2 = [sb(s2, f"p2lg{i}", (128, 16), F32) for i in range(2)]
                sm2_ = [sb(s2, f"p2sm1{i}", (128, 1), F32)
                        for i in range(2)]
                m2c = [sb(s2, f"p2m2c{i}", (128, 1), F32) for i in range(2)]
                wsm2 = [sb(s2, f"p2wsm{i}", (128, 16), F32)
                        for i in range(2)]
                i12 = [sb(s2, f"p2i1{i}", (128, 16), F32) for i in range(2)]
                i22 = [sb(s2, f"p2i2{i}", (128, 16), F32) for i in range(2)]
                mk2 = [sb(s2, f"p2mk{i}", (128, 16), F32) for i in range(2)]
                tmpf = sb(s2, "p2tmp", (128, 1024), F32)
                se = ExitStack()
                psM2 = [psum(se, f"p2psM{i}", (128, 1024), F32)
                        for i in range(2)]
                psEO2 = [psum(se, f"p2psEO{i}", (128, 1024), F32)
                         for i in range(2)]
                hm2 = [sb(se, f"p2hm{i}", (128, 4 * 1024), BF16)
                       for i in range(2)]
                Bt2 = [sb(se, f"p2Bt{i}", (128, 1024), BF16)
                       for i in range(2)]
                tmp2 = [sb(se, f"p2tmp{i}", (128, 1024), F32)
                        for i in range(2)]
                for cc in range(NCC):
                    g.dma("sp", xb[:, cc * 1024:(cc + 1) * 1024],
                          sBb[b, cc * 128:(cc + 1) * 128, t0:t0 + 1024])
                for gc in range(NCC):
                    psH_ = psM2[gc % 2]
                    for th in range(2):
                        g.mm(psH_[:, th * 512:(th + 1) * 512],
                             [(wm[:, cc * DM + gc * 128:
                                  cc * DM + (gc + 1) * 128],
                               xb[:, cc * 1024 + th * 512:
                                  cc * 1024 + th * 512 + 512])
                              for cc in range(NCC)])
                    g.op("dve", "tensor_scalar",
                         hb[:, gc * 1024:(gc + 1) * 1024],
                         [psH_[:], bmcol[:]],
                         out=hb[:, gc * 1024:(gc + 1) * 1024], in0=psH_[:],
                         scalar1=bmcol[:, gc:gc + 1], scalar2=None,
                         op0=ALU.add)
                for gc in range(NCC):
                    for i0 in range(0, 8, 2):
                        for par in range(2):
                            i = i0 + par
                            lg_, sm1_ = lg2[par], sm2_[par]
                            i1_, i2_, mk_ = i12[par], i22[par], mk2[par]
                            psL_ = psM2[par][:, 0:16]
                            g.mm(psL_, [(hb[:, gc * 1024 + i * 128:
                                            gc * 1024 + (i + 1) * 128],
                                         embt[:])])
                            g.op("dve", "tensor_copy", lg_[:],
                                 [psM2[par][:]], out=lg_[:], in_=psL_)
                            g.op("dve", "reduce_max", sm1_[:], [lg_[:]],
                                 out=sm1_[:], in_=lg_[:], axis=AX.X,
                                 negate=True)
                            g.op("act", "activation", wsm2[par][:],
                                 [lg_[:], sm1_[:]], out=wsm2[par][:],
                                 in_=lg_[:], func=AF.Exp, bias=sm1_[:],
                                 accum_out=sm1_[:])
                            g.op("dve", "tensor_scalar", i1_[:],
                                 [lg_[:], sm1_[:]], out=i1_[:], in0=lg_[:],
                                 scalar1=sm1_[:], scalar2=0.0, op0=ALU.add,
                                 op1=ALU.is_ge)
                            g.op("dve", "scalar_tensor_tensor", mk_[:],
                                 [i1_[:], lg_[:]], out=mk_[:], in0=i1_[:],
                                 scalar=-1e30, in1=lg_[:], op0=ALU.mult,
                                 op1=ALU.add)
                            g.op("dve", "reduce_max", m2c[par][:], [mk_[:]],
                                 out=m2c[par][:], in_=mk_[:], axis=AX.X,
                                 negate=True)
                            g.op("dve", "tensor_scalar", i2_[:],
                                 [mk_[:], m2c[par][:]], out=i2_[:],
                                 in0=mk_[:], scalar1=m2c[par][:],
                                 scalar2=0.0, op0=ALU.add, op1=ALU.is_ge)
                            g.op("dve", "tensor_add", i1_[:],
                                 [i1_[:], i2_[:]], out=i1_[:], in0=i1_[:],
                                 in1=i2_[:])
                        for par in range(2):
                            i = i0 + par
                            sm1_ = sm2_[par]
                            i1_, mk_ = i12[par], mk2[par]
                            g.op("dve", "reciprocal", sm1_[:], [sm1_[:]],
                                 out=sm1_[:], in_=sm1_[:])
                            g.op("dve", "tensor_scalar_mul", wsm2[par][:],
                                 [wsm2[par][:], sm1_[:]], out=wsm2[par][:],
                                 in0=wsm2[par][:], scalar1=sm1_[:])
                            g.op("dve", "tensor_mul", mk_[:],
                                 [wsm2[par][:], i1_[:]], out=mk_[:],
                                 in0=wsm2[par][:], in1=i1_[:])
                            psCT_ = psM2[par][0:16, 0:128]
                            g.tr(psCT_, mk_[:], eyef[:])
                            g.op("dve", "tensor_copy",
                                 cT[:, i * 128:(i + 1) * 128],
                                 [psM2[par][:]],
                                 out=cT[:, i * 128:(i + 1) * 128],
                                 in_=psCT_)
                    for th in range(2):
                        g.mm(psEO2[0][:, th * 512:(th + 1) * 512],
                             [(b2t[:], cT[:, th * 512:(th + 1) * 512])])
                    g.op("dve", "tensor_copy",
                         acc[:, gc * 1024:(gc + 1) * 1024], [psEO2[0][:]],
                         out=acc[:, gc * 1024:(gc + 1) * 1024],
                         in_=psEO2[0][:])

                    def emit_Bhmid(e):
                        ep = e % 2
                        for th in range(2):
                            g.mm(psM2[ep][:, th * 512:(th + 1) * 512],
                                 [(oh16[:, e * 128:(e + 1) * 128],
                                   cT[:, th * 512:(th + 1) * 512])])
                        g.op("dve", "tensor_copy", Bt2[ep][:],
                             [psM2[ep][:]], out=Bt2[ep][:],
                             in_=psM2[ep][:])
                        for ic in range(4):
                            ip = ic % 2
                            for th in range(2):
                                g.mm(psM2[ip][:, th * 512:(th + 1) * 512],
                                     [(w1[:, e * 512 + ic * 128:
                                          e * 512 + (ic + 1) * 128],
                                       hb[:, gc * 1024 + th * 512:
                                          gc * 1024 + th * 512 + 512])])
                            g.op("act", "activation",
                                 hm2[ep][:, ic * 1024:(ic + 1) * 1024],
                                 [psM2[ip][:], b1col[:]],
                                 out=hm2[ep][:, ic * 1024:(ic + 1) * 1024],
                                 in_=psM2[ip][:], func=AF.Gelu,
                                 bias=b1col[:, e * 4 + ic:e * 4 + ic + 1])

                    def emit_eo(e):
                        ep = e % 2
                        for th in range(2):
                            g.mm(psEO2[ep][:, th * 512:(th + 1) * 512],
                                 [(w2[:, (e * 4 + ic) * 128:
                                      (e * 4 + ic + 1) * 128],
                                   hm2[ep][:, ic * 1024 + th * 512:
                                           ic * 1024 + th * 512 + 512])
                                  for ic in range(4)])
                        g.op("dve", "tensor_tensor", tmp2[ep][:],
                             [psEO2[ep][:], Bt2[ep][:]], out=tmp2[ep][:],
                             in0=psEO2[ep][:], in1=Bt2[ep][:], op=ALU.mult)
                        g.op("pool", "tensor_add",
                             acc[:, gc * 1024:(gc + 1) * 1024],
                             [acc[:], tmp2[ep][:]],
                             out=acc[:, gc * 1024:(gc + 1) * 1024],
                             in0=acc[:, gc * 1024:(gc + 1) * 1024],
                             in1=tmp2[ep][:])

                    emit_Bhmid(0)
                    for e in range(1, 16):
                        emit_Bhmid(e)
                        emit_eo(e - 1)
                    emit_eo(15)
                g.barrier()
                se.close()
                for gc in range(NCC):
                    g.op("act", "copy", hb[:, gc * 1024:(gc + 1) * 1024],
                         [acc[:]], out=hb[:, gc * 1024:(gc + 1) * 1024],
                         in_=acc[:, gc * 1024:(gc + 1) * 1024])
                with _scoped(g) as sm_:
                    sbf = acc
                    xres = sb(sm_, "p2xres", (128, 1024), F32)
                    bufs = {"onescol": onescol, "ones_k1": ones_k1,
                            "epscol": epscol,
                            "sq": sb(sm_, "p2sq", (128, 1024), F32),
                            "meanr": sb(sm_, "p2mr", (1, 1024), F32),
                            "m2r": sb(sm_, "p2m2", (1, 1024), F32),
                            "varr": sb(sm_, "p2vr", (1, 1024), F32),
                            "lnf": sb(sm_, "p2lnf", (128, 1024), F32)}
                    lnb = sb(sm_, "p2lnb", (128, 1024), BF16)
                    psH = psum(sm_, "p2psH", (128, 1024), F32)
                    psS2_ = psum(sm_, "p2psS", (1, 1024), F32)
                    psQ2_ = psum(sm_, "p2psQ", (1, 1024), F32)
                    psB1_ = psum(sm_, "p2psB1", (128, 1024), F32)
                    for dc in range(NCC):
                        for th in range(2):
                            g.mm(psH[:, th * 512:(th + 1) * 512],
                                 [(wo[:, cc * DM + dc * 128:
                                      cc * DM + (dc + 1) * 128],
                                   hb[:, cc * 1024 + th * 512:
                                      cc * 1024 + th * 512 + 512])
                                  for cc in range(NCC)])
                        g.dma("sp", xres[:],
                              sB[b, dc * 128:(dc + 1) * 128, t0:t0 + 1024])
                        g.op("dve", "tensor_scalar", tmpf[:],
                             [psH[:], bocol[:]], out=tmpf[:], in0=psH[:],
                             scalar1=bocol[:, dc:dc + 1], scalar2=None,
                             op0=ALU.add)
                        g.op("dve", "tensor_add",
                             sbf[:, dc * 1024:(dc + 1) * 1024],
                             [tmpf[:], xres[:]],
                             out=sbf[:, dc * 1024:(dc + 1) * 1024],
                             in0=tmpf[:], in1=xres[:])

                    def emit(cc, sl):
                        bsl = lnb[:]
                        g.op("act", "copy", bsl, [sl], out=bsl, in_=sl)
                        g.dma("sp", sA[b, cc * 128:(cc + 1) * 128,
                                       t0:t0 + 1024], sl)
                        g.dma("act", sAb[b, cc * 128:(cc + 1) * 128,
                                         t0:t0 + 1024], bsl)

                    build_layernorm(g, bufs, sbf[:], psS2_, psQ2_, psB1_,
                                    psH, emit)
    g.barrier()


def _stage_p3(nc, g, es, sb, psum, dram, win, cin, gl, eyeb):
    sA, sAb, sB, sBb = dram["sA"], dram["sAb"], dram["sB"], dram["sBb"]
    onescol, ones_k1, epscol = gl["onescol"], gl["ones_k1"], gl["epscol"]
    for b in range(NB):
        with _scoped(g) as st:
            qraw = sb(st, "p3q", (128, NCC * T), BF16)
            kraw = sb(st, "p3k", (128, 2 * T), BF16)
            vraw = sb(st, "p3v", (128, 2 * T), BF16)
            yb = sb(st, "p3yb", (128, NCC * T), BF16)
            with _scoped(g) as s2:
                xb = sb(s2, "p3xb", (128, NCC * T), BF16)
                wq = sb(s2, "p3wq", (128, NCC * DM), BF16)
                wk = sb(s2, "p3wk", (128, NCC * 256), BF16)
                wv = sb(s2, "p3wv", (128, NCC * 256), BF16)
                cosq = sb(s2, "p3cq", (128, T), BF16)
                sinq = sb(s2, "p3sq", (128, T), F32)
                cosk = sb(s2, "p3ck", (128, T), BF16)
                sink = sb(s2, "p3sk", (128, T), F32)
                perm = sb(s2, "p3pm", (128, 128), BF16)
                t1b = sb(s2, "p3t1", (128, 512), BF16)
                t2b = sb(s2, "p3t2", (128, 512), BF16)
                psP = psum(s2, "p3psP", (128, 512), F32)
                psW = psum(s2, "p3psW", (128, 512), F32)
                stg3 = sb(s2, "p3stg", (128, 2048), F32)

                def lc3(dst, srcv, blocks, bw):
                    gsz = max(1, 2048 // bw)
                    for q in range(0, blocks, gsz):
                        nb = min(gsz, blocks - q)
                        w_ = nb * bw
                        g.dma("sp",
                              stg3[:, 0:w_].rearrange("p (n w) -> p n w",
                                                      n=nb),
                              srcv[:, q:q + nb, :])
                        g.op("dve", "tensor_copy",
                             dst[:, q * bw:q * bw + w_],
                             [stg3[:]], out=dst[:, q * bw:q * bw + w_],
                             in_=stg3[:, 0:w_])

                lc3(wq[:], win["L1_attn_Wq"].rearrange(
                    "(cc p) d -> p cc d", p=128), NCC, DM)
                lc3(wk[:], win["L1_attn_Wk"].rearrange(
                    "(cc p) d -> p cc d", p=128), NCC, 256)
                lc3(wv[:], win["L1_attn_Wv"].rearrange(
                    "(cc p) d -> p cc d", p=128), NCC, 256)
                g.dma("sp", cosq[:], cin["c_cosq"])
                g.dma("sp", sinq[:], cin["c_sinq"])
                g.dma("sp", cosk[:], cin["c_cos"])
                g.dma("sp", sink[:], cin["c_sin"])
                g.dma("sp", perm[:], cin["c_perm"])
                for cc in range(NCC):
                    g.dma("sp", xb[:, cc * T:(cc + 1) * T],
                          sAb[b, cc * 128:(cc + 1) * 128, :])
                for dst, w_, nw, cosx, sinx in [
                        (qraw, wq, NCC, cosq, sinq),
                        (kraw, wk, 2, cosk, sink), (vraw, wv, 2, None, None)]:
                    for gc in range(nw):
                        for tq in range(4):
                            g.mm(psP[:],
                                 [(w_[:, cc * (nw * 128) + gc * 128:
                                      cc * (nw * 128) + (gc + 1) * 128],
                                   xb[:, cc * T + tq * 512:
                                      cc * T + (tq + 1) * 512])
                                  for cc in range(NCC)])
                            sl = dst[:, gc * T + tq * 512:
                                     gc * T + (tq + 1) * 512]
                            g.op("dve", "tensor_copy", sl, [psP[:]],
                                 out=sl, in_=psP[:])
                            if cosx is not None:
                                g.mm(psW[:], [(perm[:], sl)])
                                g.op("dve", "tensor_tensor", t1b[:],
                                     [psW[:], sinx[:]], out=t1b[:],
                                     in0=psW[:],
                                     in1=sinx[:, tq * 512:(tq + 1) * 512],
                                     op=ALU.mult)
                                g.op("dve", "tensor_tensor", t2b[:],
                                     [sl, cosx[:]], out=t2b[:], in0=sl,
                                     in1=cosx[:, tq * 512:(tq + 1) * 512],
                                     op=ALU.mult)
                                g.op("dve", "tensor_add", sl,
                                     [t1b[:], t2b[:]], out=sl, in0=t1b[:],
                                     in1=t2b[:])
            with _scoped(g) as s2:
                kh = sb(s2, "p3kh", (64, T), BF16)
                vh = sb(s2, "p3vh", (64, T), BF16)
                vT = sb(s2, "p3vT", (128, 16 * 64), BF16)
                qh = sb(s2, "p3qh", (64, T), BF16)
                maskT4 = sb(s2, "p3mt4", (128, 2048), F32)
                onebc = sb(s2, "p3onebc", (128, 1), BF16)
                probT2 = [sb(s2, f"p3pT{i}", (128, 16 * 512), BF16)
                          for i in range(2)]
                tmpd = sb(s2, "p3tmpd", (128, 512), F32)
                sqb = sb(s2, "p3sqb", (64, T), F32)
                nrow = sb(s2, "p3nrow", (1, T), F32)
                nk = sb(s2, "p3nk", (1, 1), F32)
                nq = sb(s2, "p3nq", (1, 1), F32)
                mcol = sb(s2, "p3mcol", (128, 1), F32)
                denr = [sb(s2, f"p3dr{i}", (1, 512), F32) for i in range(2)]
                rb = sb(s2, "p3rb", (64, 512), F32)
                psS = psum(s2, "p3psS", (128, 512), F32)
                psS2 = psum(s2, "p3psS2", (128, 512), F32)
                psY2 = [psum(s2, f"p3psY{i}", (64, 512), F32)
                        for i in range(2)]
                psDen = [psum(s2, f"p3psD{i}", (1, 512), F32)
                         for i in range(2)]
                psVT = psum(s2, "p3psVT", (128, 64), BF16)
                psB = psum(s2, "p3psB", (128, 512), F32)
                g.dma("sp", maskT4[:], cin["c_maskT4"])
                g.op("dve", "memset", onebc[:], [], ap=onebc[:],
                     constant=1.0)

                def colnorm_max(src64, out11):
                    g.op("act", "activation", sqb[:], [src64], out=sqb[:],
                         in_=src64, func=AF.Square)
                    for sgm in range(4):
                        g.mm(psB[0:1, 0:512],
                             [(onescol[0:64, :],
                               sqb[:, sgm * 512:(sgm + 1) * 512])])
                        g.op("dve", "tensor_copy",
                             nrow[:, sgm * 512:(sgm + 1) * 512], [psB[:]],
                             out=nrow[:, sgm * 512:(sgm + 1) * 512],
                             in_=psB[0:1, 0:512])
                    g.op("dve", "reduce_max", out11, [nrow[:]], out=out11,
                         in_=nrow[:], axis=AX.X, negate=False)
                    g.op("act", "activation", out11, [out11], out=out11,
                         in_=out11, func=AF.Sqrt)

                for j in range(4):
                    base = (j % 2) * 64
                    g.op("dve", "tensor_copy", kh[:],
                         [kraw[:]], out=kh[:],
                         in_=kraw[base:base + 64, (j // 2) * T:
                                  (j // 2) * T + T])
                    g.op("dve", "tensor_copy", vh[:],
                         [vraw[:]], out=vh[:],
                         in_=vraw[base:base + 64, (j // 2) * T:
                                  (j // 2) * T + T])
                    for c in range(16):
                        g.tr(psVT[:], vh[:, c * 128:(c + 1) * 128],
                             eyeb[0:64, 0:64])
                        g.op("dve", "tensor_copy", vT[:, c * 64:(c + 1) * 64],
                             [psVT[:]], out=vT[:, c * 64:(c + 1) * 64],
                             in_=psVT[:])
                    colnorm_max(kh[:], nk[:])
                    for h in range(4 * j, 4 * j + 4):
                        qbase = (h % 2) * 64
                        g.op("dve", "tensor_copy", qh[:], [qraw[:]],
                             out=qh[:],
                             in_=qraw[qbase:qbase + 64,
                                      (h // 2) * T:(h // 2) * T + T])
                        colnorm_max(qh[:], nq[:])
                        g.op("dve", "tensor_mul", nq[:], [nq[:], nk[:]],
                             out=nq[:], in0=nq[:], in1=nk[:])
                        g.op("dve", "tensor_scalar", nq[:], [nq[:]],
                             out=nq[:], in0=nq[:], scalar1=-1.0,
                             scalar2=-1.0, op0=ALU.mult, op1=ALU.add)
                        g.mm(psB[:, 0:1], [(ones_k1[:], nq[:])])
                        g.op("dve", "tensor_copy", mcol[:], [psB[:]],
                             out=mcol[:], in_=psB[:, 0:1])

                        def p3_ST(tq):
                            qp = tq % 2
                            pT = probT2[qp]
                            nkc = (tq + 1) * 4
                            for c in range(nkc):
                                ps = psS if c % 2 == 0 else psS2
                                g.mm(ps[:],
                                     [(kh[:, c * 128:(c + 1) * 128],
                                       qh[:, tq * 512:(tq + 1) * 512])])
                                m = c - tq * 4
                                if m >= 0:
                                    g.op("dve", "tensor_add", tmpd[:],
                                         [ps[:], maskT4[:]], out=tmpd[:],
                                         in0=ps[:],
                                         in1=maskT4[:, m * 512:
                                                    (m + 1) * 512])
                                    g.op("act", "activation",
                                         pT[:, c * 512:(c + 1) * 512],
                                         [tmpd[:], mcol[:]],
                                         out=pT[:, c * 512:(c + 1) * 512],
                                         in_=tmpd[:], func=AF.Exp,
                                         bias=mcol[:])
                                else:
                                    g.op("act", "activation",
                                         pT[:, c * 512:(c + 1) * 512],
                                         [ps[:], mcol[:]],
                                         out=pT[:, c * 512:(c + 1) * 512],
                                         in_=ps[:], func=AF.Exp,
                                         bias=mcol[:])

                        def p3_denav(tq):
                            qp = tq % 2
                            pT = probT2[qp]
                            psY = psY2[qp]
                            nkc = (tq + 1) * 4
                            items = []
                            for c in range(nkc):
                                items.append(
                                    (psDen[qp][:], onebc[:],
                                     pT[:, c * 512:(c + 1) * 512],
                                     c == 0, c == nkc - 1))
                                items.append(
                                    (psY[:], vT[:, c * 64:(c + 1) * 64],
                                     pT[:, c * 512:(c + 1) * 512],
                                     c == 0, c == nkc - 1))
                            g.mm_multi(items,
                                       outs=[psDen[qp][:], psY[:]],
                                       ins=[onebc[:], vT[:], pT[:]])
                            g.op("dve", "tensor_copy", denr[qp][:],
                                 [psDen[qp][:]], out=denr[qp][:],
                                 in_=psDen[qp][:])
                            g.op("dve", "tensor_scalar_max", denr[qp][:],
                                 [denr[qp][:]], out=denr[qp][:],
                                 in0=denr[qp][:], scalar1=1e-38)
                            g.op("dve", "reciprocal", denr[qp][:],
                                 [denr[qp][:]], out=denr[qp][:],
                                 in_=denr[qp][:])
                            g.mm(psB[0:64, :], [(ones_k1[:, 0:64],
                                                 denr[qp][:])])
                            g.op("act", "copy", rb[:], [psB[:]], out=rb[:],
                                 in_=psB[0:64, :])
                            dsl = yb[qbase:qbase + 64,
                                     (h // 2) * T + tq * 512:
                                     (h // 2) * T + (tq + 1) * 512]
                            g.op("dve", "tensor_tensor", dsl,
                                 [psY[:], rb[:]], out=dsl, in0=psY[:],
                                 in1=rb[:], op=ALU.mult)

                        p3_ST(0)
                        for tq in range(1, 4):
                            p3_ST(tq)
                            p3_denav(tq - 1)
                        p3_denav(3)
            with _scoped(g) as s3:
                wo = sb(s3, "p3wo", (128, NCC * DM), BF16)
                stg4 = sb(s3, "p3stg4", (128, 2048), F32)
                wov = win["L1_attn_Wo"].rearrange(
                    "(cc p) d -> p cc d", p=128)
                for q in range(0, NCC, 2):
                    g.dma("sp",
                          stg4[:].rearrange("p (n w) -> p n w", n=2),
                          wov[:, q:q + 2, :])
                    g.op("dve", "tensor_copy",
                         wo[:, q * DM:(q + 2) * DM],
                         [stg4[:]], out=wo[:, q * DM:(q + 2) * DM],
                         in_=stg4[:])
                sbf = sb(s3, "p3s", (128, NCC * 1024), F32)
                xres = sb(s3, "p3xres", (128, 1024), F32)
                tmpf = sb(s3, "p3tmpf", (128, 1024), F32)
                psO = psum(s3, "p3psO", (128, 1024), F32)
                psQ2 = psum(s3, "p3psQ2", (128, 1024), F32)
                psB1 = psum(s3, "p3psB1", (128, 1024), F32)
                psB2 = psum(s3, "p3psB2", (128, 1024), F32)
                bufs = {"onescol": onescol, "ones_k1": ones_k1,
                        "epscol": epscol,
                        "sq": sb(s3, "p3sq2", (128, 1024), F32),
                        "meanr": sb(s3, "p3mr", (1, 1024), F32),
                        "m2r": sb(s3, "p3m2", (1, 1024), F32),
                        "varr": sb(s3, "p3vr", (1, 1024), F32),
                        "lnf": sb(s3, "p3lnf", (128, 1024), F32)}
                lnb = sb(s3, "p3lnb", (128, 1024), BF16)
                for hf in range(2):
                    t0 = hf * 1024
                    for dc in range(NCC):
                        for th in range(2):
                            g.mm(psO[:, th * 512:(th + 1) * 512],
                                 [(wo[:, cc * DM + dc * 128:
                                      cc * DM + (dc + 1) * 128],
                                   yb[:, cc * T + t0 + th * 512:
                                      cc * T + t0 + th * 512 + 512])
                                  for cc in range(NCC)])
                        g.dma("sp", xres[:],
                              sA[b, dc * 128:(dc + 1) * 128,
                                 t0:t0 + 1024])
                        g.op("dve", "tensor_add",
                             sbf[:, dc * 1024:(dc + 1) * 1024],
                             [psO[:], xres[:]],
                             out=sbf[:, dc * 1024:(dc + 1) * 1024],
                             in0=psO[:], in1=xres[:])

                    def emit(cc, sl, t0=t0):
                        bsl = lnb[:]
                        g.op("act", "copy", bsl, [sl], out=bsl, in_=sl)
                        g.dma("sp", sB[b, cc * 128:(cc + 1) * 128,
                                       t0:t0 + 1024], sl)
                        g.dma("act", sBb[b, cc * 128:(cc + 1) * 128,
                                         t0:t0 + 1024], bsl)

                    build_layernorm(g, bufs, sbf[:], psO, psQ2, psB1,
                                    psB2, emit)
    g.barrier()


def _stage_p4(nc, g, es, sb, psum, dram, win, cin, gl, eyeb, eyef):
    sB, sBb, y = dram["sB"], dram["sBb"], dram["y"]
    onescol, ones_k1, epscol = gl["onescol"], gl["ones_k1"], gl["epscol"]
    with _scoped(g) as st:
        rwT = sb(st, "p4rw", (128, NCC * 7), BF16)
        oh7 = sb(st, "p4oh", (7, 896), BF16)
        g.dma("sp", oh7[:], cin["c_oh7"])
        with _scoped(g) as sr:
            rtmp = sb(sr, "p4rt", (7, DM), BF16)
            psR7 = psum(sr, "p4psR7", (128, 128), BF16)
            g.dma("pool", rtmp[:], win["L1_ffn_router_w"])
            for cc in range(NCC):
                g.tr(psR7[:, 0:7], rtmp[:, cc * 128:(cc + 1) * 128],
                     eyeb[0:7, 0:7])
                g.op("dve", "tensor_copy", rwT[:, cc * 7:(cc + 1) * 7],
                     [psR7[:]], out=rwT[:, cc * 7:(cc + 1) * 7],
                     in_=psR7[:, 0:7])

        for tt in range(4):
            b, t0 = tt // 2, (tt % 2) * 1024
            with _scoped(g) as s2:
                xb = sb(s2, "p4xb", (128, NCC * 1024), BF16)
                hb = sb(s2, "p4hb", (128, 32 * 1024), BF16)
                acc = sb(s2, "p4acc", (128, NCC * 1024), F32)
                cT = sb(s2, "p4cT", (7, 1024), BF16)
                Bt = sb(s2, "p4Bt", (128, 1024), BF16)
                tmpf = sb(s2, "p4tmpf", (128, 1024), F32)
                lg = sb(s2, "p4lg", (128, 7), F32)
                sm1 = sb(s2, "p4sm1", (128, 1), F32)
                sm2 = sb(s2, "p4sm2", (128, 1), F32)
                w1c = sb(s2, "p4w1c", (128, 1), F32)
                i1 = sb(s2, "p4i1", (128, 7), F32)
                i2 = sb(s2, "p4i2", (128, 7), F32)
                mk = sb(s2, "p4mk", (128, 7), F32)
                cwb = sb(s2, "p4cwb", (128, 7), BF16)
                ser = ExitStack()
                psL = psum(ser, "p4psL", (128, 7), F32)
                psCT = psum(ser, "p4psCT", (16, 128), BF16)
                for cc in range(NCC):
                    g.dma("sp", xb[:, cc * 1024:(cc + 1) * 1024],
                          sBb[b, cc * 128:(cc + 1) * 128, t0:t0 + 1024])
                for i in range(8):
                    g.mm(psL[:], [(xb[:, cc * 1024 + i * 128:
                                      cc * 1024 + (i + 1) * 128],
                                   rwT[:, cc * 7:(cc + 1) * 7])
                                  for cc in range(NCC)])
                    g.op("dve", "tensor_copy", lg[:], [psL[:]], out=lg[:],
                         in_=psL[:])
                    g.op("dve", "reduce_max", sm1[:], [lg[:]], out=sm1[:],
                         in_=lg[:], axis=AX.X, negate=True)
                    g.op("dve", "tensor_scalar", i1[:], [lg[:], sm1[:]],
                         out=i1[:], in0=lg[:], scalar1=sm1[:], scalar2=None,
                         op0=ALU.add)
                    g.op("act", "activation", i1[:], [i1[:]], out=i1[:],
                         in_=i1[:], func=AF.Sign)
                    g.op("dve", "tensor_scalar_add", i1[:], [i1[:]],
                         out=i1[:], in0=i1[:], scalar1=1.0)
                    g.op("dve", "scalar_tensor_tensor", mk[:], [i1[:], lg[:]],
                         out=mk[:], in0=i1[:], scalar=-1e30, in1=lg[:],
                         op0=ALU.mult, op1=ALU.add)
                    g.op("dve", "reduce_max", sm2[:], [mk[:]], out=sm2[:],
                         in_=mk[:], axis=AX.X, negate=True)
                    g.op("dve", "tensor_scalar", i2[:], [mk[:], sm2[:]],
                         out=i2[:], in0=mk[:], scalar1=sm2[:], scalar2=None,
                         op0=ALU.add)
                    g.op("act", "activation", i2[:], [i2[:]], out=i2[:],
                         in_=i2[:], func=AF.Sign)
                    g.op("dve", "tensor_scalar_add", i2[:], [i2[:]],
                         out=i2[:], in0=i2[:], scalar1=1.0)
                    g.op("dve", "tensor_sub", w1c[:], [sm2[:], sm1[:]],
                         out=w1c[:], in0=sm2[:], in1=sm1[:])
                    g.op("act", "activation", w1c[:], [w1c[:]], out=w1c[:],
                         in_=w1c[:], func=AF.Sigmoid)
                    g.op("dve", "tensor_scalar_mul", i1[:], [i1[:], w1c[:]],
                         out=i1[:], in0=i1[:], scalar1=w1c[:])
                    g.op("dve", "tensor_scalar", w1c[:], [w1c[:]],
                         out=w1c[:], in0=w1c[:], scalar1=-1.0, scalar2=1.0,
                         op0=ALU.mult, op1=ALU.add)
                    g.op("dve", "tensor_scalar_mul", i2[:], [i2[:], w1c[:]],
                         out=i2[:], in0=i2[:], scalar1=w1c[:])
                    g.op("dve", "tensor_add", cwb[:], [i1[:], i2[:]],
                         out=cwb[:], in0=i1[:], in1=i2[:])
                    g.tr(psCT[0:7, :], cwb[:], eyeb[:])
                    g.op("dve", "tensor_copy", cT[:, i * 128:(i + 1) * 128],
                         [psCT[:]], out=cT[:, i * 128:(i + 1) * 128],
                         in_=psCT[0:7, :])
                g.barrier()
                ser.close()
                psG2 = [psum(s2, f"p4psG{i}", (128, 1024), F32)
                        for i in range(2)]
                psU2 = [psum(s2, f"p4psU{i}", (128, 1024), F32)
                        for i in range(2)]
                sg2 = [sb(s2, f"p4sg2{i}", (128, 1024), BF16)
                       for i in range(2)]
                sw_ = ExitStack()
                wg_b = [sb(sw_, f"p4wg{i}", (128, 4096), BF16)
                        for i in range(2)]
                wu_b = [sb(sw_, f"p4wu{i}", (128, 4096), BF16)
                        for i in range(2)]
                wd_b = [sb(sw_, f"p4wd{i}", (128, 4096), BF16)
                        for i in range(2)]
                for ei, e in enumerate([7] + list(range(7))):
                    if e == 7:
                        gim, uim = dram["sg_s"], dram["su_s"]
                        dim_ = dram["sd_s"]
                    else:
                        gim, uim = dram["gate_s"][e], dram["up_s"][e]
                        dim_ = dram["down_s"][e]
                    for hs in range(8):
                        pp = hs % 2
                        g.dma("sp", wg_b[pp][:], gim[hs])
                        g.dma("act", wu_b[pp][:], uim[hs])
                        for hl in range(4):
                            hc = hs * 4 + hl
                            hp = hl % 2
                            items = []
                            for th in range(2):
                                for cc in range(NCC):
                                    items.append(
                                        (psG2[hp][:, th * 512:
                                                   (th + 1) * 512],
                                         wg_b[pp][:, cc * 512 + hl * 128:
                                                  cc * 512 +
                                                  (hl + 1) * 128],
                                         xb[:, cc * 1024 + th * 512:
                                            cc * 1024 + th * 512 + 512],
                                         cc == 0, cc == NCC - 1))
                                for cc in range(NCC):
                                    items.append(
                                        (psU2[hp][:, th * 512:
                                                   (th + 1) * 512],
                                         wu_b[pp][:, cc * 512 + hl * 128:
                                                  cc * 512 +
                                                  (hl + 1) * 128],
                                         xb[:, cc * 1024 + th * 512:
                                            cc * 1024 + th * 512 + 512],
                                         cc == 0, cc == NCC - 1))
                            g.mm_multi(items,
                                       outs=[psG2[hp][:], psU2[hp][:]],
                                       ins=[wg_b[pp][:], wu_b[pp][:],
                                            xb[:]])
                            g.op("act", "activation", sg2[hp][:],
                                 [psG2[hp][:]], out=sg2[hp][:],
                                 in_=psG2[hp][:], func=AF.Silu)
                            g.op("dve", "tensor_tensor",
                                 hb[:, hc * 1024:(hc + 1) * 1024],
                                 [psU2[hp][:], sg2[hp][:]],
                                 out=hb[:, hc * 1024:(hc + 1) * 1024],
                                 in0=psU2[hp][:], in1=sg2[hp][:],
                                 op=ALU.mult)
                    if e != 7:
                        for th in range(2):
                            g.mm(psU2[1][:, th * 512:(th + 1) * 512],
                                 [(oh7[:, e * 128:(e + 1) * 128],
                                   cT[:, th * 512:(th + 1) * 512])])
                        g.op("act", "copy", Bt[:], [psU2[1][:]], out=Bt[:],
                             in_=psU2[1][:])
                    psD4 = [psG2[0], psG2[1], psU2[0], psU2[1]]
                    for p_ in range(2):
                        for hr in range(4):
                            g.dma("sp", wd_b[hr % 2][:], dim_[p_, hr])
                            items = []
                            for hcc in range(8):
                                for dc in range(4):
                                    for th in range(2):
                                        items.append(
                                            (psD4[dc][:, th * 512:
                                                      (th + 1) * 512],
                                             wd_b[hr % 2][:,
                                                          hcc * 512 +
                                                          dc * 128:
                                                          hcc * 512 +
                                                          (dc + 1) * 128],
                                             hb[:,
                                                (hr * 8 + hcc) * 1024 +
                                                th * 512:
                                                (hr * 8 + hcc) * 1024 +
                                                th * 512 + 512],
                                             hr == 0 and hcc == 0,
                                             hr == 3 and hcc == 7))
                            g.mm_multi(items, outs=[p[:] for p in psD4],
                                       ins=[wd_b[hr % 2][:], hb[:]])
                        for dc in range(4):
                            d_ = p_ * 4 + dc
                            asl = acc[:, d_ * 1024:(d_ + 1) * 1024]
                            if e == 7:
                                g.op("dve", "tensor_copy", asl,
                                     [psD4[dc][:]], out=asl,
                                     in_=psD4[dc][:])
                            else:
                                g.op("dve", "tensor_tensor",
                                     tmpf[:], [psD4[dc][:], Bt[:]],
                                     out=tmpf[:], in0=psD4[dc][:],
                                     in1=Bt[:], op=ALU.mult)
                                g.op("pool", "tensor_add", asl,
                                     [acc[:], tmpf[:]], out=asl, in0=asl,
                                     in1=tmpf[:])
                g.barrier()
                sw_.close()
                with _scoped(g) as s3:
                    xres = sb(s3, "p4xres", (128, 1024), F32)
                    obuf = sb(s3, "p4ob", (128, 2 * 1024), F32)
                    lnf2 = sb(s3, "p4lnf2", (128, NCC * 1024), F32)
                    bufs = {"onescol": onescol, "ones_k1": ones_k1,
                            "epscol": epscol,
                            "sq": sb(s3, "p4sq", (128, 1024), F32),
                            "meanr": sb(s3, "p4mr", (1, 1024), F32),
                            "m2r": sb(s3, "p4m2", (1, 1024), F32),
                            "varr": sb(s3, "p4vr", (1, 1024), F32),
                            "lnf": sb(s3, "p4lnf", (128, 1024), F32)}
                    for dc in range(NCC):
                        g.dma("sp", xres[:],
                              sB[b, dc * 128:(dc + 1) * 128, t0:t0 + 1024])
                        g.op("dve", "tensor_add",
                             acc[:, dc * 1024:(dc + 1) * 1024],
                             [acc[:], xres[:]],
                             out=acc[:, dc * 1024:(dc + 1) * 1024],
                             in0=acc[:, dc * 1024:(dc + 1) * 1024],
                             in1=xres[:])

                    def emit(cc, sl):
                        g.op("pool", "tensor_copy",
                             lnf2[:, cc * 1024:(cc + 1) * 1024], [sl],
                             out=lnf2[:, cc * 1024:(cc + 1) * 1024], in_=sl)

                    build_layernorm(g, bufs, acc[:], psU2[0], psU2[1],
                                    psG2[0], psG2[1], emit)
                    for t8 in range(8):
                        ob = obuf[:, (t8 % 2) * 1024:(t8 % 2 + 1) * 1024]
                        for cc in range(NCC):
                            ps = psG2[0] if cc % 2 == 0 else psG2[1]
                            g.tr(ps[:, 0:128],
                                 lnf2[:, cc * 1024 + t8 * 128:
                                      cc * 1024 + (t8 + 1) * 128], eyef[:])
                            g.op("dve", "tensor_copy",
                                 ob[:, cc * 128:(cc + 1) * 128], [ps[:]],
                                 out=ob[:, cc * 128:(cc + 1) * 128],
                                 in_=ps[:, 0:128])
                        g.dma("sp", y[b, t0 + t8 * 128:t0 + (t8 + 1) * 128,
                                      :], ob)
    g.barrier()


_BUILD_CACHE = {}


def _prep_weight(name, v):
    vv = np.ascontiguousarray(v, dtype=np.float32)
    if name in ("L0_attn_time_mix_k", "L0_attn_time_mix_v",
                "L0_attn_time_mix_r"):
        vv = vv.reshape(-1)
    return vv


def kernel(**inputs):
    stages = tuple(DEV_STAGES)
    key = (stages, DEV_TAPS)
    if key not in _BUILD_CACHE:
        _BUILD_CACHE[key] = build_program(stages, DEV_TAPS)
    nc, cst, declared = _BUILD_CACHE[key]

    x = np.ascontiguousarray(inputs["L0_attn_x"])
    base = {}
    for name, v in inputs.items():
        if name != "L0_attn_x" and name in declared:
            base[name] = _prep_weight(name, v)
    for name, v in cst.items():
        if name in declared:
            base[name] = v
    in_maps = []
    for core in range(8):
        m = dict(base)
        if "p0" in stages:
            m["xin"] = x[2 * core:2 * core + 2]
        in_maps.append(m)
    res = run_bass_kernel_spmd(nc, in_maps, core_ids=list(range(8)))
    out = np.concatenate([np.asarray(r["y"]) for r in res.results], axis=0)
    return out.astype(np.float32)


# version 07
# speedup vs baseline: 1.0890x; 1.0890x over previous
import numpy as np
import ml_dtypes
from contextlib import ExitStack

import concourse.bass as bass
import concourse.mybir as mybir
from concourse.bass_utils import run_bass_kernel_spmd

F32 = mybir.dt.float32
BF16 = mybir.dt.bfloat16
AF = mybir.ActivationFunctionType
ALU = mybir.AluOpType
AX = mybir.AxisListType

NB = 2
T = 2048
DM = 1024
NCC = DM // 128

ALL_STAGES = ("p0", "p1", "p2", "p3", "p4")
DEV_STAGES = ALL_STAGES
DEV_TAPS = False

ENGS = ["pe", "act", "dve", "pool", "sp"]
DMA_ENGS = ["sp", "pool", "act"]


def _tname(ap):
    t = getattr(ap, "tensor", None)
    if t is None:
        return None
    return t.name if hasattr(t, "name") else str(t)


class G:

    def __init__(self, nc, es):
        self.nc = nc
        self.chain = {e: es.enter_context(nc.semaphore(f"chain_{e}"))
                      for e in ENGS}
        self.cnt = {e: 0 for e in ENGS}
        self.dma_sem = {e: es.enter_context(nc.semaphore(f"dma_{e}"))
                        for e in DMA_ENGS}
        self.dma_cnt = {e: 0 for e in DMA_ENGS}
        self.waited = {e: {} for e in ENGS}
        self.state = {}
        self.q = {"pe": nc.tensor, "act": nc.scalar, "dve": nc.vector,
                  "pool": nc.gpsimd, "sp": nc.sync}

    def _sem(self, semname):
        kind, e = semname.split("_", 1)
        return self.chain[e] if kind == "chain" else self.dma_sem[e]

    def _wait(self, eng, semname, val):
        if val <= 0:
            return
        w = self.waited[eng]
        if w.get(semname, 0) >= val:
            return
        self.q[eng].wait_ge(self._sem(semname), val)
        w[semname] = val

    def pre(self, eng, in_aps, out_aps):
        need = {}
        for ap in in_aps:
            st = self.state.get(_tname(ap))
            if st:
                for s, v in st["w"].items():
                    need[s] = max(need.get(s, 0), v)
        for ap in out_aps:
            st = self.state.get(_tname(ap))
            if st:
                for s, v in st["w"].items():
                    need[s] = max(need.get(s, 0), v)
                for s, v in st["r"].items():
                    need[s] = max(need.get(s, 0), v)
        for s, v in sorted(need.items()):
            self._wait(eng, s, v)

    def post(self, eng, ins, in_aps, out_aps, dma=False):
        if dma:
            self.dma_cnt[eng] += 16
            val = self.dma_cnt[eng]
            ins.then_inc(self.dma_sem[eng], 16)
            semname = f"dma_{eng}"
        else:
            self.cnt[eng] += 1
            val = self.cnt[eng]
            ins.then_inc(self.chain[eng], 1)
            semname = f"chain_{eng}"
        for ap in out_aps:
            n = _tname(ap)
            if n is not None:
                st = self.state.setdefault(n, {"w": {}, "r": {}})
                st["w"][semname] = max(st["w"].get(semname, 0), val)
        for ap in in_aps:
            n = _tname(ap)
            if n is not None:
                st = self.state.setdefault(n, {"w": {}, "r": {}})
                st["r"][semname] = max(st["r"].get(semname, 0), val)
        return ins

    def op(self, eng, fname, o_, i_, **kw):
        outs = o_ if isinstance(o_, list) else [o_]
        self.pre(eng, i_, outs)
        ins_obj = getattr(self.q[eng], fname)(**kw)
        self.post(eng, ins_obj, i_, outs)
        return ins_obj

    def dma(self, eng, out, in_):
        self.pre(eng, [in_], [out])
        ins_obj = self.q[eng].dma_start(out=out, in_=in_)
        self.post(eng, ins_obj, [in_], [out], dma=True)
        return ins_obj

    def mm(self, psum_out, pairs, extra_ins=()):
        in_aps = [ap for p in pairs for ap in p] + list(extra_ins)
        self.pre("pe", in_aps, [psum_out])
        n = len(pairs)
        ins_obj = None
        for i, (l, r) in enumerate(pairs):
            ins_obj = self.q["pe"].matmul(psum_out, l, r, start=(i == 0),
                                          stop=(i == n - 1))
        self.post("pe", ins_obj, in_aps, [psum_out])
        return ins_obj

    def mm_multi(self, items, outs, ins):
        self.pre("pe", ins, outs)
        ins_obj = None
        for ps, l, r, st, sp in items:
            ins_obj = self.q["pe"].matmul(ps, l, r, start=st, stop=sp)
        self.post("pe", ins_obj, ins, outs)
        return ins_obj

    def tr(self, psum_out, in_ap, identity):
        self.pre("pe", [in_ap, identity], [psum_out])
        ins_obj = self.q["pe"].transpose(out=psum_out, in_=in_ap,
                                         identity=identity)
        self.post("pe", ins_obj, [in_ap, identity], [psum_out])
        return ins_obj

    def barrier(self, full=False):
        dq = DMA_ENGS if full else [p for p in DMA_ENGS if p != "pool"]
        for e in ENGS:
            for p in ENGS:
                if p != e:
                    self._wait(e, f"chain_{p}", self.cnt[p])
            for p in dq:
                self._wait(e, f"dma_{p}", self.dma_cnt[p])


def host_constants():
    c = {}
    c["c_eyeb"] = np.eye(128, dtype=ml_dtypes.bfloat16)
    c["c_eyef"] = np.eye(128, dtype=np.float32)
    m = np.zeros((128, 128), np.float32)
    m[np.triu_indices(128, 1)] = -1e38
    c["c_mask"] = m
    half = 32
    inv_freq = 1.0 / 10000.0 ** (np.arange(half, dtype=np.float64) / half)
    t = np.arange(T, dtype=np.float64)
    freqs = np.outer(t, inv_freq)
    emb = np.concatenate([freqs, freqs], 1)
    cos = np.cos(emb).T
    sin = np.sin(emb).T
    cos2 = np.concatenate([cos, cos], 0)
    sin2 = np.concatenate([sin, sin], 0)
    c["c_cos"] = cos2.astype(ml_dtypes.bfloat16)
    c["c_sin"] = sin2.astype(np.float32)
    c["c_cosq"] = (cos2 * 0.125).astype(ml_dtypes.bfloat16)
    c["c_sinq"] = (sin2 * 0.125).astype(np.float32)
    P = np.zeros((128, 128), np.float64)
    for i in range(128):
        base = (i // 64) * 64
        j = i % 64
        if j < 32:
            P[i, base + j + 32] = -1.0
        else:
            P[i, base + j - 32] = 1.0
    c["c_perm"] = P.T.astype(ml_dtypes.bfloat16)
    oh16 = np.zeros((16, 16 * 128), np.float32)
    for e in range(16):
        oh16[e, e * 128:(e + 1) * 128] = 1.0
    c["c_oh16"] = oh16.astype(ml_dtypes.bfloat16)
    mt4 = np.zeros((128, 4, 512), np.float32)
    for m in range(4):
        for i in range(128):
            mt4[i, m, :min(512, m * 128 + i)] = -1e38
    c["c_maskT4"] = mt4.reshape(128, 2048)
    oh7 = np.zeros((7, 7 * 128), np.float32)
    for e in range(7):
        oh7[e, e * 128:(e + 1) * 128] = 1.0
    c["c_oh7"] = oh7.astype(ml_dtypes.bfloat16)
    return c


def _load_cols(g, dst, src_ap, n, eng="act"):
    v = src_ap.rearrange("(n p) -> n p", p=128)
    for i in range(n):
        g.dma(eng, dst[:, i:i + 1], v[i])


def build_layernorm(g, bufs, s, psS, psQ, psB1, psB2, emit_out):
    nc = g.nc
    onescol, ones_k1, epscol = bufs["onescol"], bufs["ones_k1"], bufs["epscol"]
    sq = bufs["sq"]
    items = []
    for cc in range(NCC):
        g.op("act", "activation", sq[:],
             [s[:, cc * 1024:(cc + 1) * 1024]],
             out=sq[:],
             in_=s[:, cc * 1024:(cc + 1) * 1024], func=AF.Square)
        for th in range(2):
            g.mm_multi(
                [(psS[0:1, th * 512:(th + 1) * 512], onescol[:],
                  s[:, cc * 1024 + th * 512: cc * 1024 + (th + 1) * 512],
                  cc == 0, cc == NCC - 1)],
                outs=[psS[0:1, :]], ins=[onescol[:], s[:]])
            g.mm_multi(
                [(psQ[0:1, th * 512:(th + 1) * 512], onescol[:],
                  sq[:, th * 512:th * 512 + 512],
                  cc == 0, cc == NCC - 1)],
                outs=[psQ[0:1, :]], ins=[onescol[:], sq[:]])
    meanr, m2r, varr = bufs["meanr"], bufs["m2r"], bufs["varr"]
    g.op("dve", "tensor_scalar", meanr[:], [psS[0:1, :]], out=meanr[:],
         in0=psS[0:1, :], scalar1=1.0 / DM, scalar2=None, op0=ALU.mult)
    g.op("dve", "tensor_scalar", m2r[:], [psQ[0:1, :]], out=m2r[:],
         in0=psQ[0:1, :], scalar1=1.0 / DM, scalar2=None, op0=ALU.mult)
    g.op("act", "activation", varr[:], [meanr[:]], out=varr[:], in_=meanr[:],
         func=AF.Square)
    g.op("dve", "tensor_sub", varr[:], [m2r[:], varr[:]], out=varr[:],
         in0=m2r[:], in1=varr[:])
    g.op("dve", "tensor_scalar_max", varr[:], [varr[:]], out=varr[:],
         in0=varr[:], scalar1=0.0)
    g.op("act", "activation", varr[:], [varr[:], epscol[:]], out=varr[:],
         in_=varr[:], func=AF.Sqrt, bias=epscol[:])
    g.op("dve", "reciprocal", varr[:], [varr[:]], out=varr[:], in_=varr[:])
    for th in range(2):
        g.mm(psB1[:, th * 512:(th + 1) * 512],
             [(ones_k1[:], meanr[0:1, th * 512:(th + 1) * 512])])
        g.mm(psB2[:, th * 512:(th + 1) * 512],
             [(ones_k1[:], varr[0:1, th * 512:(th + 1) * 512])])
    lnf = bufs["lnf"]
    for cc in range(NCC):
        sl = lnf[:]
        g.op("dve", "tensor_sub", sl, [s[:, cc * 1024:(cc + 1) * 1024],
                                       psB1[:]],
             out=sl, in0=s[:, cc * 1024:(cc + 1) * 1024], in1=psB1[:])
        g.op("dve", "tensor_mul", sl, [sl, psB2[:]], out=sl, in0=sl,
             in1=psB2[:])
        emit_out(cc, sl)


def build_program(stages=ALL_STAGES, dev_taps=False):
    nc = bass.Bass("TRN2", target_bir_lowering=False)
    cst = host_constants()

    dram = {}

    def dt_(name, shape, dtype, kind):
        dram[name] = nc.dram_tensor(name, list(shape), dtype, kind=kind).ap()
        return dram[name]

    first = stages[0]
    stream_reader = {"p1": "sA", "p2": "sB", "p3": "sA", "p4": "sB"}
    stream_reader_b = {"p2": "sBb", "p3": "sAb", "p4": "sBb"}
    written = set()
    for s_ in stages:
        if s_ == "p0":
            written.add("sA")
        elif s_ == "p1":
            written.update(["sB", "sBb"])
        elif s_ == "p2":
            written.update(["sA", "sAb"])
        elif s_ == "p3":
            written.update(["sB", "sBb"])
    needed_in = set()
    if first != "p0":
        needed_in.add(stream_reader[first])
        if first in stream_reader_b:
            needed_in.add(stream_reader_b[first])
    needed_in -= written

    def stream_kind(name):
        if name in needed_in:
            return "ExternalInput"
        if dev_taps and name in written:
            return "ExternalOutput"
        return "Internal"

    if "p0" in stages:
        dt_("xin", (NB, T, DM), F32, "ExternalInput")
    for name in ["sA", "sB"]:
        if name in written or name in needed_in:
            dt_(name, (NB, DM, T), F32, stream_kind(name))
    for name in ["sAb", "sBb"]:
        if name in written or name in needed_in:
            dt_(name, (NB, DM, T), BF16, stream_kind(name))
    if "p4" in stages:
        dt_("y", (NB, T, DM), F32, "ExternalOutput")

    win = {}

    def wt(name, shape):
        win[name] = nc.dram_tensor(name, list(shape), F32,
                                   kind="ExternalInput").ap()
        return win[name]

    if "p1" in stages:
        for n_ in ["L0_attn_w", "L0_attn_u"]:
            wt(n_, (DM,))
        for n_ in ["L0_attn_time_mix_k", "L0_attn_time_mix_v",
                   "L0_attn_time_mix_r"]:
            wt(n_, (DM,))
        for n_ in ["L0_attn_Wk", "L0_attn_Wv", "L0_attn_Wr", "L0_attn_Wo"]:
            wt(n_, (DM, DM))
    if "p2" in stages:
        wt("L0_ffn_Wm", (DM, DM))
        wt("L0_ffn_bm", (DM,))
        wt("L0_ffn_expert_emb", (128, 16))
        wt("L0_ffn_W1", (16, 128, 512))
        wt("L0_ffn_b1", (16, 512))
        wt("L0_ffn_W2", (16, 512, 128))
        wt("L0_ffn_b2", (16, 128))
        wt("L0_ffn_Wo", (DM, DM))
        wt("L0_ffn_bo", (DM,))
    if "p3" in stages:
        wt("L1_attn_Wq", (DM, DM))
        wt("L1_attn_Wk", (DM, 256))
        wt("L1_attn_Wv", (DM, 256))
        wt("L1_attn_Wo", (DM, DM))
    if "p4" in stages:
        wt("L1_ffn_router_w", (7, DM))
        wt("L1_ffn_w_gate", (7, DM, 4096))
        wt("L1_ffn_w_up", (7, DM, 4096))
        wt("L1_ffn_w_down", (7, 4096, DM))
        wt("L1_ffn_sw_gate", (1, DM, 4096))
        wt("L1_ffn_sw_up", (1, DM, 4096))
        wt("L1_ffn_sw_down", (1, 4096, DM))
        dt_("gate_s", (7, 8, 128, 4096), BF16, "Internal")
        dt_("up_s", (7, 8, 128, 4096), BF16, "Internal")
        dt_("down_s", (7, 2, 4, 128, 4096), BF16, "Internal")
        dt_("sg_s", (8, 128, 4096), BF16, "Internal")
        dt_("su_s", (8, 128, 4096), BF16, "Internal")
        dt_("sd_s", (2, 4, 128, 4096), BF16, "Internal")

    cin = {}
    cshape = {"c_eyeb": (128, 128), "c_eyef": (128, 128),
              "c_mask": (128, 128), "c_maskT4": (128, 2048),
              "c_cos": (128, T), "c_sin": (128, T),
              "c_cosq": (128, T), "c_sinq": (128, T), "c_perm": (128, 128),
              "c_oh16": (16, 2048), "c_oh7": (7, 896)}
    cdt = {"c_eyeb": BF16, "c_eyef": F32, "c_mask": F32, "c_maskT4": F32,
           "c_cos": BF16,
           "c_sin": F32, "c_cosq": BF16, "c_sinq": F32, "c_perm": BF16,
           "c_oh16": BF16, "c_oh7": BF16}
    for n_ in cshape:
        cin[n_] = nc.dram_tensor(n_, list(cshape[n_]), cdt[n_],
                                 kind="ExternalInput").ap()

    es = ExitStack()
    with es:
        g = G(nc, es)
        ctr = [0]

        def sb(st, name, shape, dtype):
            ctr[0] += 1
            return st.enter_context(
                nc.sbuf_tensor(f"{name}_{ctr[0]}", list(shape), dtype))

        def psum(st, name, shape, dtype=F32):
            ctr[0] += 1
            return st.enter_context(
                nc.psum_tensor(f"{name}_{ctr[0]}", list(shape), dtype))

        dummy = sb(es, "dummy_w", (128, 8), F32)
        eyeb = sb(es, "eyeb", (128, 128), BF16)
        eyef = sb(es, "eyef", (128, 128), F32)
        onescol = sb(es, "onescol", (128, 1), F32)
        ones_k1 = sb(es, "ones_k1", (1, 128), F32)
        epscol = sb(es, "epscol", (1, 1), F32)
        convb = sb(es, "convb", (128, 2 * 4096), BF16)

        gl = {"onescol": onescol[:], "ones_k1": ones_k1[:],
              "epscol": epscol[:]}

        block = es.enter_context(nc.Block())

        @block.sync
        def _(_sync):
            _build_body(nc, g, es, sb, psum, dram, win, cin, stages, gl,
                        dummy, eyeb, eyef, convb)

    declared = set()
    if "p0" in stages:
        declared.add("xin")
    declared |= set(win.keys()) | set(cin.keys())
    declared |= needed_in
    return nc, cst, declared


def _build_body(nc, g, es, sb, psum, dram, win, cin, stages, gl, dummy,
                eyeb, eyef, convb):
    onescol = gl["onescol"]
    ones_k1 = gl["ones_k1"]
    epscol = gl["epscol"]

    g.op("act", "memzero", dummy[:], [], ap=dummy[:])
    for fn in [AF.Copy, AF.Exp, AF.Sigmoid, AF.Silu, AF.Gelu, AF.Sign,
               AF.Square, AF.Sqrt]:
        g.op("act", "activation", dummy[:], [dummy[:]], out=dummy[:],
             in_=dummy[:], func=fn)
    g.op("dve", "memset", onescol[:], [], ap=onescol[:], constant=1.0)
    g.op("dve", "memset", ones_k1[:], [], ap=ones_k1[:], constant=1.0)
    g.op("dve", "memset", epscol[:], [], ap=epscol[:], constant=1e-5)
    g.dma("sp", eyeb[:], cin["c_eyeb"])
    g.dma("sp", eyef[:], cin["c_eyef"])

    if "p4" in stages:
        ci = [0]

        def conv_gu(src_ap, dst_ap):
            v = src_ap.rearrange("(cc p) h -> p cc h", p=128)
            for hs in range(8):
                buf = convb[:, (ci[0] % 2) * 4096:(ci[0] % 2) * 4096 + 4096]
                ci[0] += 1
                g.dma("pool", buf.rearrange("p (cc h) -> p cc h", cc=8),
                      v[:, :, hs * 512:(hs + 1) * 512])
                g.dma("pool", dst_ap[hs], buf)

        def conv_dn(src_ap, dst_ap):
            for p_ in range(2):
                for hr in range(4):
                    buf = convb[:, (ci[0] % 2) * 4096:
                                (ci[0] % 2) * 4096 + 4096]
                    ci[0] += 1
                    v = src_ap[hr * 1024:(hr + 1) * 1024,
                               p_ * 512:(p_ + 1) * 512].rearrange(
                        "(hcc p) d -> p hcc d", p=128)
                    g.dma("pool",
                          buf.rearrange("p (hcc d) -> p hcc d", hcc=8), v)
                    g.dma("pool", dst_ap[p_, hr], buf)

        for e in range(7):
            conv_gu(win["L1_ffn_w_gate"][e], dram["gate_s"][e])
            conv_gu(win["L1_ffn_w_up"][e], dram["up_s"][e])
            conv_dn(win["L1_ffn_w_down"][e], dram["down_s"][e])
        conv_gu(win["L1_ffn_sw_gate"][0], dram["sg_s"])
        conv_gu(win["L1_ffn_sw_up"][0], dram["su_s"])
        conv_dn(win["L1_ffn_sw_down"][0], dram["sd_s"])

    if "p0" in stages:
        _stage_p0(nc, g, es, sb, psum, dram, eyef)
    if "p1" in stages:
        _stage_p1(nc, g, es, sb, psum, dram, win, gl, eyeb)
    if "p2" in stages:
        _stage_p2(nc, g, es, sb, psum, dram, win, cin, gl, eyeb, eyef)
    if "p3" in stages:
        _stage_p3(nc, g, es, sb, psum, dram, win, cin, gl, eyeb)
    if "p4" in stages:
        _stage_p4(nc, g, es, sb, psum, dram, win, cin, gl, eyeb, eyef)
    g.barrier(full=True)


from contextlib import contextmanager


@contextmanager
def _scoped(g):
    st = ExitStack()
    try:
        with st:
            yield st
            g.barrier()
    finally:
        pass


def _stage_p0(nc, g, es, sb, psum, dram, eyef):
    xin, sA = dram["xin"], dram["sA"]
    with _scoped(g) as st:
        xt = [sb(st, f"p0x{i}", (128, DM), F32) for i in range(4)]
        asm = [sb(st, f"p0a{c}", (128, 512), F32) for c in range(NCC)]
        psT = [psum(st, f"p0p{i}", (128, 512), F32) for i in range(2)]
        for b in range(NB):
            for tg in range(4):
                t0 = tg * 512
                for ti in range(4):
                    g.dma("sp", xt[ti][:],
                          xin[b, t0 + ti * 128:t0 + (ti + 1) * 128, :])
                for cc in range(NCC):
                    ps = psT[cc % 2]
                    for ti in range(4):
                        g.tr(ps[:, ti * 128:(ti + 1) * 128],
                             xt[ti][:, cc * 128:(cc + 1) * 128], eyef[:])
                    g.op("dve", "tensor_copy", asm[cc][:], [ps[:]],
                         out=asm[cc][:], in_=ps[:])
                    g.dma("act", sA[b, cc * 128:(cc + 1) * 128, t0:t0 + 512],
                          asm[cc][:])
    g.barrier()


def _stage_p1(nc, g, es, sb, psum, dram, win, gl, eyeb):
    sA, sB, sBb = dram["sA"], dram["sB"], dram["sBb"]
    onescol, ones_k1, epscol = gl["onescol"], gl["ones_k1"], gl["epscol"]
    with _scoped(g) as st:
        wT = {}
        for nm in ["Wk", "Wv", "Wr", "Wo"]:
            wT[nm] = sb(st, "p1" + nm, (128, NCC * DM), BF16)
        with _scoped(g) as sw:
            wtmp = sb(sw, "p1wtmp", (128, DM), BF16)
            wtmpf = sb(sw, "p1wtmpf", (128, DM), F32)
            psWb = psum(sw, "p1psw", (128, 512), BF16)
            for nm in ["Wk", "Wv", "Wr", "Wo"]:
                W = win["L0_attn_" + nm]
                for dc in range(NCC):
                    g.dma("sp", wtmpf[:], W[dc * 128:(dc + 1) * 128, :])
                    g.op("dve", "tensor_copy", wtmp[:], [wtmpf[:]],
                         out=wtmp[:], in_=wtmpf[:])
                    for cc in range(NCC):
                        ps = psWb[:, (cc % 4) * 128:(cc % 4 + 1) * 128]
                        g.tr(ps, wtmp[:, cc * 128:(cc + 1) * 128], eyeb[:])
                        g.op("dve", "tensor_copy",
                             wT[nm][:, cc * DM + dc * 128:
                                    cc * DM + (dc + 1) * 128],
                             [psWb[:]],
                             out=wT[nm][:, cc * DM + dc * 128:
                                        cc * DM + (dc + 1) * 128],
                             in_=ps)
            g.barrier()
        ucol = sb(st, "p1u", (128, NCC), F32)
        edec = sb(st, "p1edec", (128, NCC), F32)
        tmcol = {}
        _load_cols(g, edec, win["L0_attn_w"], NCC)
        g.op("act", "activation", edec[:], [edec[:]], out=edec[:],
             in_=edec[:], func=AF.Exp)
        g.op("dve", "tensor_scalar_mul", edec[:], [edec[:]], out=edec[:],
             in0=edec[:], scalar1=-1.0)
        g.op("act", "activation", edec[:], [edec[:]], out=edec[:],
             in_=edec[:], func=AF.Exp)
        _load_cols(g, ucol, win["L0_attn_u"], NCC)
        for nm in ["k", "v", "r"]:
            tmcol[nm] = sb(st, "p1tm" + nm, (128, NCC), F32)
            _load_cols(g, tmcol[nm], win["L0_attn_time_mix_" + nm], NCC)

        ones1024 = sb(st, "p1ones", (128, 1024), F32)
        g.op("dve", "memset", ones1024[:], [], ap=ones1024[:], constant=1.0)
        carryA = sb(st, "p1cA", (128, NCC), F32)
        carryB = sb(st, "p1cB", (128, NCC), F32)
        negMs = sb(st, "p1nM", (128, NCC), F32)
        colt = sb(st, "p1colt", (128, 1), F32)

        psK = psum(st, "p1psK", (128, 1024), F32)
        psV = psum(st, "p1psV", (128, 1024), F32)
        psR = psum(st, "p1psR", (128, 1024), F32)
        psO = psum(st, "p1psO", (128, 1024), F32)

        for b in range(NB):
            g.op("dve", "memset", carryA[:], [], ap=carryA[:], constant=0.0)
            g.op("dve", "memset", carryB[:], [], ap=carryB[:], constant=0.0)
            for hf in range(2):
                t0 = hf * 1024
                with _scoped(g) as sh:
                    mixK = sb(sh, "p1mK", (128, NCC * 1024), BF16)
                    mixV = sb(sh, "p1mV", (128, NCC * 1024), BF16)
                    mixR = sb(sh, "p1mR", (128, NCC * 1024), BF16)
                    rwkvb = sb(sh, "p1rw", (128, NCC * 1024), BF16)
                    with _scoped(g) as sm:
                        xc = sb(sm, "p1xc", (128, 1025), F32)
                        tdiff = sb(sm, "p1td", (128, 1024), F32)
                        for cc in range(NCC):
                            if hf == 0:
                                g.op("dve", "memset", xc[:, 0:1], [],
                                     ap=xc[:, 0:1], constant=0.0)
                                g.dma("sp", xc[:, 1:1025],
                                      sA[b, cc * 128:(cc + 1) * 128,
                                         t0:t0 + 1024])
                            else:
                                g.dma("sp", xc[:],
                                      sA[b, cc * 128:(cc + 1) * 128,
                                         t0 - 1:t0 + 1024])
                            g.op("dve", "tensor_sub", tdiff[:],
                                 [xc[:]], out=tdiff[:], in0=xc[:, 1:1025],
                                 in1=xc[:, 0:1024])
                            for nm, mx in [("k", mixK), ("v", mixV),
                                           ("r", mixR)]:
                                sl = mx[:, cc * 1024:(cc + 1) * 1024]
                                g.op("dve", "scalar_tensor_tensor", sl,
                                     [tdiff[:], tmcol[nm][:], xc[:]],
                                     out=sl, in0=tdiff[:],
                                     scalar=tmcol[nm][:, cc:cc + 1],
                                     in1=xc[:, 0:1024], op0=ALU.mult,
                                     op1=ALU.add)
                    with _scoped(g) as sm:
                        kg = sb(sm, "p1kg", (128, 1024), F32)
                        vg = sb(sm, "p1vg", (128, 1024), F32)
                        sigr = sb(sm, "p1sr", (128, 1024), F32)
                        zk = sb(sm, "p1zk", (128, 1024), F32)
                        zkv = sb(sm, "p1zkv", (128, 1024), F32)
                        decb = sb(sm, "p1dec", (128, 1024), F32)
                        aS = sb(sm, "p1aS", (128, 1024), F32)
                        bS = sb(sm, "p1bS", (128, 1024), F32)
                        den = sb(sm, "p1den", (128, 1024), F32)
                        num = sb(sm, "p1num", (128, 1024), F32)
                        negM = sb(sm, "p1negM", (128, 1), F32)
                        ubias = sb(sm, "p1ub", (128, 1), F32)
                        for gc in range(NCC):
                            for ps_, mx, dst in [(psK, mixK, kg),
                                                 (psV, mixV, vg)]:
                                for th in range(2):
                                    g.mm(ps_[:, th * 512:(th + 1) * 512],
                                         [(wT["Wk" if dst is kg else "Wv"]
                                           [:, cc * DM + gc * 128:
                                            cc * DM + (gc + 1) * 128],
                                           mx[:, cc * 1024 + th * 512:
                                              cc * 1024 + th * 512 + 512])
                                          for cc in range(NCC)])
                                g.op("dve", "tensor_copy", dst[:], [ps_[:]],
                                     out=dst[:], in_=ps_[:])
                            for th in range(2):
                                g.mm(psR[:, th * 512:(th + 1) * 512],
                                     [(wT["Wr"][:, cc * DM + gc * 128:
                                                cc * DM + (gc + 1) * 128],
                                       mixR[:, cc * 1024 + th * 512:
                                            cc * 1024 + th * 512 + 512])
                                      for cc in range(NCC)])
                            g.op("act", "activation", sigr[:], [psR[:]],
                                 out=sigr[:], in_=psR[:], func=AF.Sigmoid)
                            g.op("dve", "reduce_max", negM[:], [kg[:]],
                                 out=negM[:], in_=kg[:], axis=AX.X,
                                 negate=True)
                            if hf == 1:
                                g.op("dve", "tensor_tensor", negM[:],
                                     [negM[:], negMs[:]], out=negM[:],
                                     in0=negM[:],
                                     in1=negMs[:, gc:gc + 1], op=ALU.min)
                                g.op("dve", "tensor_sub", colt[:],
                                     [negM[:], negMs[:]], out=colt[:],
                                     in0=negM[:],
                                     in1=negMs[:, gc:gc + 1])
                                g.op("act", "activation", colt[:], [colt[:]],
                                     out=colt[:], in_=colt[:], func=AF.Exp)
                                g.op("dve", "tensor_mul",
                                     carryA[:, gc:gc + 1],
                                     [carryA[:], colt[:]],
                                     out=carryA[:, gc:gc + 1],
                                     in0=carryA[:, gc:gc + 1], in1=colt[:])
                                g.op("dve", "tensor_mul",
                                     carryB[:, gc:gc + 1],
                                     [carryB[:], colt[:]],
                                     out=carryB[:, gc:gc + 1],
                                     in0=carryB[:, gc:gc + 1], in1=colt[:])
                            else:
                                g.op("dve", "tensor_copy",
                                     negMs[:, gc:gc + 1], [negM[:]],
                                     out=negMs[:, gc:gc + 1], in_=negM[:])
                            g.op("act", "activation", zk[:], [kg[:], negM[:]],
                                 out=zk[:], in_=kg[:], func=AF.Exp,
                                 bias=negM[:])
                            g.op("dve", "tensor_mul", zkv[:], [zk[:], vg[:]],
                                 out=zkv[:], in0=zk[:], in1=vg[:])
                            g.op("dve", "tensor_scalar_mul", decb[:],
                                 [ones1024[:], edec[:]], out=decb[:],
                                 in0=ones1024[:],
                                 scalar1=edec[:, gc:gc + 1])
                            g.op("dve", "tensor_tensor_scan", aS[:],
                                 [decb[:], zkv[:], carryA[:]], out=aS[:],
                                 data0=decb[:], data1=zkv[:],
                                 initial=carryA[:, gc:gc + 1],
                                 op0=ALU.mult, op1=ALU.add)
                            g.op("dve", "tensor_tensor_scan", bS[:],
                                 [decb[:], zk[:], carryB[:]], out=bS[:],
                                 data0=decb[:], data1=zk[:],
                                 initial=carryB[:, gc:gc + 1],
                                 op0=ALU.mult, op1=ALU.add)
                            g.op("dve", "tensor_add", ubias[:],
                                 [ucol[:], negM[:]], out=ubias[:],
                                 in0=ucol[:, gc:gc + 1], in1=negM[:])
                            g.op("act", "activation", zk[:],
                                 [kg[:], ubias[:]], out=zk[:], in_=kg[:],
                                 func=AF.Exp, bias=ubias[:])
                            g.op("dve", "tensor_mul", zkv[:], [zk[:], vg[:]],
                                 out=zkv[:], in0=zk[:], in1=vg[:])
                            g.op("dve", "tensor_add", den[:, 1:1024],
                                 [bS[:], zk[:]], out=den[:, 1:1024],
                                 in0=bS[:, 0:1023], in1=zk[:, 1:1024])
                            g.op("dve", "tensor_add", den[:, 0:1],
                                 [carryB[:], zk[:]], out=den[:, 0:1],
                                 in0=carryB[:, gc:gc + 1], in1=zk[:, 0:1])
                            g.op("dve", "tensor_scalar_max", den[:], [den[:]],
                                 out=den[:], in0=den[:], scalar1=1e-38)
                            g.op("dve", "reciprocal", den[:], [den[:]],
                                 out=den[:], in_=den[:])
                            g.op("dve", "tensor_add", num[:, 1:1024],
                                 [aS[:], zkv[:]], out=num[:, 1:1024],
                                 in0=aS[:, 0:1023], in1=zkv[:, 1:1024])
                            g.op("dve", "tensor_add", num[:, 0:1],
                                 [carryA[:], zkv[:]], out=num[:, 0:1],
                                 in0=carryA[:, gc:gc + 1], in1=zkv[:, 0:1])
                            g.op("dve", "tensor_copy", carryA[:, gc:gc + 1],
                                 [aS[:]], out=carryA[:, gc:gc + 1],
                                 in_=aS[:, 1023:1024])
                            g.op("dve", "tensor_copy", carryB[:, gc:gc + 1],
                                 [bS[:]], out=carryB[:, gc:gc + 1],
                                 in_=bS[:, 1023:1024])
                            g.op("dve", "tensor_mul", num[:], [num[:], den[:]],
                                 out=num[:], in0=num[:], in1=den[:])
                            g.op("dve", "tensor_mul",
                                 rwkvb[:, gc * 1024:(gc + 1) * 1024],
                                 [sigr[:], num[:]],
                                 out=rwkvb[:, gc * 1024:(gc + 1) * 1024],
                                 in0=sigr[:], in1=num[:])
                    with _scoped(g) as sm:
                        sbf = sb(sm, "p1s", (128, NCC * 1024), F32)
                        xres = sb(sm, "p1xres", (128, 1024), F32)
                        bufs = {"onescol": onescol, "ones_k1": ones_k1,
                                "epscol": epscol,
                                "sq": sb(sm, "p1sq", (128, 1024), F32),
                                "meanr": sb(sm, "p1mr", (1, 1024), F32),
                                "m2r": sb(sm, "p1m2", (1, 1024), F32),
                                "varr": sb(sm, "p1vr", (1, 1024), F32),
                                "lnf": sb(sm, "p1lnf", (128, 1024), F32)}
                        lnb = sb(sm, "p1lnb", (128, 1024), BF16)
                        for dc in range(NCC):
                            for th in range(2):
                                g.mm(psO[:, th * 512:(th + 1) * 512],
                                     [(wT["Wo"][:, cc * DM + dc * 128:
                                                cc * DM + (dc + 1) * 128],
                                       rwkvb[:, cc * 1024 + th * 512:
                                             cc * 1024 + th * 512 + 512])
                                      for cc in range(NCC)])
                            g.dma("sp", xres[:],
                                  sA[b, dc * 128:(dc + 1) * 128,
                                     t0:t0 + 1024])
                            g.op("dve", "tensor_add",
                                 sbf[:, dc * 1024:(dc + 1) * 1024],
                                 [psO[:], xres[:]],
                                 out=sbf[:, dc * 1024:(dc + 1) * 1024],
                                 in0=psO[:], in1=xres[:])

                        def emit(cc, sl):
                            bsl = lnb[:]
                            g.op("act", "copy", bsl, [sl], out=bsl, in_=sl)
                            g.dma("sp", sB[b, cc * 128:(cc + 1) * 128,
                                           t0:t0 + 1024], sl)
                            g.dma("act", sBb[b, cc * 128:(cc + 1) * 128,
                                             t0:t0 + 1024], bsl)

                        build_layernorm(g, bufs, sbf[:], psK, psV, psR, psO,
                                        emit)
    g.barrier()


def _stage_p2(nc, g, es, sb, psum, dram, win, cin, gl, eyeb, eyef):
    sB, sBb, sA, sAb = dram["sB"], dram["sBb"], dram["sA"], dram["sAb"]
    onescol, ones_k1, epscol = gl["onescol"], gl["ones_k1"], gl["epscol"]
    with _scoped(g) as st:
        wm = sb(st, "p2wm", (128, NCC * DM), BF16)
        wo = sb(st, "p2wo", (128, NCC * DM), BF16)
        w1 = sb(st, "p2w1", (128, 16 * 512), BF16)
        w2 = sb(st, "p2w2", (128, 16 * 4 * 128), BF16)
        embt = sb(st, "p2emb", (128, 16), BF16)
        b2t = sb(st, "p2b2", (16, 128), BF16)
        oh16 = sb(st, "p2oh", (16, 2048), BF16)
        bmcol = sb(st, "p2bm", (128, NCC), F32)
        bocol = sb(st, "p2bo", (128, NCC), F32)
        b1col = sb(st, "p2b1", (128, 64), F32)
        stg = sb(st, "p2stg", (128, 2048), F32)

        def lc2(dst, srcv, blocks, bw):
            gsz = max(1, 2048 // bw)
            for q in range(0, blocks, gsz):
                nb = min(gsz, blocks - q)
                w_ = nb * bw
                g.dma("sp",
                      stg[:, 0:w_].rearrange("p (n w) -> p n w", n=nb),
                      srcv[:, q:q + nb, :])
                g.op("dve", "tensor_copy", dst[:, q * bw:q * bw + w_],
                     [stg[:]], out=dst[:, q * bw:q * bw + w_],
                     in_=stg[:, 0:w_])

        lc2(wm[:], win["L0_ffn_Wm"].rearrange(
            "(cc p) d -> p cc d", p=128), NCC, DM)
        lc2(wo[:], win["L0_ffn_Wo"].rearrange(
            "(cc p) d -> p cc d", p=128), NCC, DM)
        lc2(w1[:], win["L0_ffn_W1"].rearrange("e d i -> d e i"), 16, 512)
        lc2(w2[:], win["L0_ffn_W2"].rearrange(
            "e (ic p) o -> p (e ic) o", p=128), 64, 128)
        embf = sb(st, "p2embf", (128, 16), F32)
        g.dma("sp", embf[:], win["L0_ffn_expert_emb"])
        g.op("dve", "tensor_copy", embt[:], [embf[:]], out=embt[:],
             in_=embf[:])
        g.dma("sp", stg[0:16, 0:128], win["L0_ffn_b2"])
        g.op("dve", "tensor_copy", b2t[:], [stg[:]], out=b2t[:],
             in_=stg[0:16, 0:128])
        g.dma("sp", oh16[:], cin["c_oh16"])
        _load_cols(g, bmcol, win["L0_ffn_bm"], NCC)
        _load_cols(g, bocol, win["L0_ffn_bo"], NCC)
        b1v = win["L0_ffn_b1"].rearrange("e (ic p) -> e ic p", p=128)
        for e in range(16):
            for ic in range(4):
                g.dma("act", b1col[:, e * 4 + ic:e * 4 + ic + 1],
                      b1v[e, ic])

        for tt in range(4):
            b, t0 = tt // 2, (tt % 2) * 1024
            with _scoped(g) as s2:
                xb = sb(s2, "p2xb", (128, NCC * 1024), BF16)
                hb = sb(s2, "p2hb", (128, NCC * 1024), BF16)
                acc = sb(s2, "p2acc", (128, NCC * 1024), F32)
                cT = sb(s2, "p2cT", (16, 1024), BF16)
                lg2 = [sb(s2, f"p2lg{i}", (128, 16), F32) for i in range(2)]
                sm2_ = [sb(s2, f"p2sm1{i}", (128, 1), F32)
                        for i in range(2)]
                m2c = [sb(s2, f"p2m2c{i}", (128, 1), F32) for i in range(2)]
                wsm2 = [sb(s2, f"p2wsm{i}", (128, 16), F32)
                        for i in range(2)]
                i12 = [sb(s2, f"p2i1{i}", (128, 16), F32) for i in range(2)]
                i22 = [sb(s2, f"p2i2{i}", (128, 16), F32) for i in range(2)]
                mk2 = [sb(s2, f"p2mk{i}", (128, 16), F32) for i in range(2)]
                tmpf = sb(s2, "p2tmp", (128, 1024), F32)
                hf32 = sb(s2, "p2hf32", (128, 1024), F32)
                lgall = sb(s2, "p2lgall", (128, 1024), F32)
                se = ExitStack()
                psM2 = [psum(se, f"p2psM{i}", (128, 1024), F32)
                        for i in range(2)]
                psEO2 = [psum(se, f"p2psEO{i}", (128, 1024), F32)
                         for i in range(2)]
                hm2 = [sb(se, f"p2hm{i}", (128, 4 * 1024), BF16)
                       for i in range(2)]
                Bt2 = [sb(se, f"p2Bt{i}", (128, 1024), BF16)
                       for i in range(2)]
                tmp2 = [sb(se, f"p2tmp{i}", (128, 1024), F32)
                        for i in range(2)]
                for cc in range(NCC):
                    g.dma("sp", xb[:, cc * 1024:(cc + 1) * 1024],
                          sBb[b, cc * 128:(cc + 1) * 128, t0:t0 + 1024])
                for gc in range(NCC):
                    psH_ = psM2[gc % 2]
                    for th in range(2):
                        g.mm(psH_[:, th * 512:(th + 1) * 512],
                             [(wm[:, cc * DM + gc * 128:
                                  cc * DM + (gc + 1) * 128],
                               xb[:, cc * 1024 + th * 512:
                                  cc * 1024 + th * 512 + 512])
                              for cc in range(NCC)])
                    g.op("dve", "tensor_scalar",
                         hb[:, gc * 1024:(gc + 1) * 1024],
                         [psH_[:], bmcol[:]],
                         out=hb[:, gc * 1024:(gc + 1) * 1024], in0=psH_[:],
                         scalar1=bmcol[:, gc:gc + 1], scalar2=None,
                         op0=ALU.add)
                    g.op("dve", "tensor_scalar", hf32[:],
                         [psH_[:], bmcol[:]], out=hf32[:], in0=psH_[:],
                         scalar1=bmcol[:, gc:gc + 1], scalar2=None,
                         op0=ALU.add)
                    for i in range(8):
                        psL_ = psEO2[i % 2][:, 0:16]
                        g.mm(psL_, [(hf32[:, i * 128:(i + 1) * 128],
                                     embf[:])])
                        g.op("dve", "tensor_copy",
                             lgall[:, (gc * 8 + i) * 16:
                                   (gc * 8 + i + 1) * 16],
                             [psEO2[i % 2][:]],
                             out=lgall[:, (gc * 8 + i) * 16:
                                       (gc * 8 + i + 1) * 16],
                             in_=psL_)
                for gc in range(NCC):
                    for i0 in range(0, 8, 2):
                        for par in range(2):
                            i = i0 + par
                            lg_, sm1_ = lg2[par], sm2_[par]
                            i1_, i2_, mk_ = i12[par], i22[par], mk2[par]
                            g.op("dve", "tensor_copy", lg_[:],
                                 [lgall[:]], out=lg_[:],
                                 in_=lgall[:, (gc * 8 + i) * 16:
                                           (gc * 8 + i + 1) * 16])
                            g.op("dve", "reduce_max", sm1_[:], [lg_[:]],
                                 out=sm1_[:], in_=lg_[:], axis=AX.X,
                                 negate=True)
                            g.op("act", "activation", wsm2[par][:],
                                 [lg_[:], sm1_[:]], out=wsm2[par][:],
                                 in_=lg_[:], func=AF.Exp, bias=sm1_[:],
                                 accum_out=sm1_[:])
                            g.op("dve", "tensor_scalar", i1_[:],
                                 [lg_[:], sm1_[:]], out=i1_[:], in0=lg_[:],
                                 scalar1=sm1_[:], scalar2=0.0, op0=ALU.add,
                                 op1=ALU.is_ge)
                            g.op("dve", "scalar_tensor_tensor", mk_[:],
                                 [i1_[:], lg_[:]], out=mk_[:], in0=i1_[:],
                                 scalar=-1e30, in1=lg_[:], op0=ALU.mult,
                                 op1=ALU.add)
                            g.op("dve", "reduce_max", m2c[par][:], [mk_[:]],
                                 out=m2c[par][:], in_=mk_[:], axis=AX.X,
                                 negate=True)
                            g.op("dve", "tensor_scalar", i2_[:],
                                 [mk_[:], m2c[par][:]], out=i2_[:],
                                 in0=mk_[:], scalar1=m2c[par][:],
                                 scalar2=0.0, op0=ALU.add, op1=ALU.is_ge)
                            g.op("dve", "tensor_add", i1_[:],
                                 [i1_[:], i2_[:]], out=i1_[:], in0=i1_[:],
                                 in1=i2_[:])
                        for par in range(2):
                            i = i0 + par
                            sm1_ = sm2_[par]
                            i1_, mk_ = i12[par], mk2[par]
                            g.op("dve", "reciprocal", sm1_[:], [sm1_[:]],
                                 out=sm1_[:], in_=sm1_[:])
                            g.op("dve", "tensor_scalar_mul", wsm2[par][:],
                                 [wsm2[par][:], sm1_[:]], out=wsm2[par][:],
                                 in0=wsm2[par][:], scalar1=sm1_[:])
                            g.op("dve", "tensor_mul", mk_[:],
                                 [wsm2[par][:], i1_[:]], out=mk_[:],
                                 in0=wsm2[par][:], in1=i1_[:])
                            psCT_ = psM2[par][0:16, 0:128]
                            g.tr(psCT_, mk_[:], eyef[:])
                            g.op("dve", "tensor_copy",
                                 cT[:, i * 128:(i + 1) * 128],
                                 [psM2[par][:]],
                                 out=cT[:, i * 128:(i + 1) * 128],
                                 in_=psCT_)
                    for th in range(2):
                        g.mm(psEO2[0][:, th * 512:(th + 1) * 512],
                             [(b2t[:], cT[:, th * 512:(th + 1) * 512])])
                    g.op("dve", "tensor_copy",
                         acc[:, gc * 1024:(gc + 1) * 1024], [psEO2[0][:]],
                         out=acc[:, gc * 1024:(gc + 1) * 1024],
                         in_=psEO2[0][:])

                    def emit_Bhmid(e):
                        ep = e % 2
                        for th in range(2):
                            g.mm(psM2[ep][:, th * 512:(th + 1) * 512],
                                 [(oh16[:, e * 128:(e + 1) * 128],
                                   cT[:, th * 512:(th + 1) * 512])])
                        g.op("dve", "tensor_copy", Bt2[ep][:],
                             [psM2[ep][:]], out=Bt2[ep][:],
                             in_=psM2[ep][:])
                        for ic in range(4):
                            ip = ic % 2
                            for th in range(2):
                                g.mm(psM2[ip][:, th * 512:(th + 1) * 512],
                                     [(w1[:, e * 512 + ic * 128:
                                          e * 512 + (ic + 1) * 128],
                                       hb[:, gc * 1024 + th * 512:
                                          gc * 1024 + th * 512 + 512])])
                            g.op("act", "activation",
                                 hm2[ep][:, ic * 1024:(ic + 1) * 1024],
                                 [psM2[ip][:], b1col[:]],
                                 out=hm2[ep][:, ic * 1024:(ic + 1) * 1024],
                                 in_=psM2[ip][:], func=AF.Gelu,
                                 bias=b1col[:, e * 4 + ic:e * 4 + ic + 1])

                    def emit_eo(e):
                        ep = e % 2
                        for th in range(2):
                            g.mm(psEO2[ep][:, th * 512:(th + 1) * 512],
                                 [(w2[:, (e * 4 + ic) * 128:
                                      (e * 4 + ic + 1) * 128],
                                   hm2[ep][:, ic * 1024 + th * 512:
                                           ic * 1024 + th * 512 + 512])
                                  for ic in range(4)])
                        g.op("dve", "tensor_tensor", tmp2[ep][:],
                             [psEO2[ep][:], Bt2[ep][:]], out=tmp2[ep][:],
                             in0=psEO2[ep][:], in1=Bt2[ep][:], op=ALU.mult)
                        g.op("pool", "tensor_add",
                             acc[:, gc * 1024:(gc + 1) * 1024],
                             [acc[:], tmp2[ep][:]],
                             out=acc[:, gc * 1024:(gc + 1) * 1024],
                             in0=acc[:, gc * 1024:(gc + 1) * 1024],
                             in1=tmp2[ep][:])

                    emit_Bhmid(0)
                    for e in range(1, 16):
                        emit_Bhmid(e)
                        emit_eo(e - 1)
                    emit_eo(15)
                g.barrier()
                se.close()
                for gc in range(NCC):
                    g.op("act", "copy", hb[:, gc * 1024:(gc + 1) * 1024],
                         [acc[:]], out=hb[:, gc * 1024:(gc + 1) * 1024],
                         in_=acc[:, gc * 1024:(gc + 1) * 1024])
                with _scoped(g) as sm_:
                    sbf = acc
                    xres = sb(sm_, "p2xres", (128, 1024), F32)
                    bufs = {"onescol": onescol, "ones_k1": ones_k1,
                            "epscol": epscol,
                            "sq": sb(sm_, "p2sq", (128, 1024), F32),
                            "meanr": sb(sm_, "p2mr", (1, 1024), F32),
                            "m2r": sb(sm_, "p2m2", (1, 1024), F32),
                            "varr": sb(sm_, "p2vr", (1, 1024), F32),
                            "lnf": sb(sm_, "p2lnf", (128, 1024), F32)}
                    lnb = sb(sm_, "p2lnb", (128, 1024), BF16)
                    psH = psum(sm_, "p2psH", (128, 1024), F32)
                    psS2_ = psum(sm_, "p2psS", (1, 1024), F32)
                    psQ2_ = psum(sm_, "p2psQ", (1, 1024), F32)
                    psB1_ = psum(sm_, "p2psB1", (128, 1024), F32)
                    for dc in range(NCC):
                        for th in range(2):
                            g.mm(psH[:, th * 512:(th + 1) * 512],
                                 [(wo[:, cc * DM + dc * 128:
                                      cc * DM + (dc + 1) * 128],
                                   hb[:, cc * 1024 + th * 512:
                                      cc * 1024 + th * 512 + 512])
                                  for cc in range(NCC)])
                        g.dma("sp", xres[:],
                              sB[b, dc * 128:(dc + 1) * 128, t0:t0 + 1024])
                        g.op("dve", "tensor_scalar", tmpf[:],
                             [psH[:], bocol[:]], out=tmpf[:], in0=psH[:],
                             scalar1=bocol[:, dc:dc + 1], scalar2=None,
                             op0=ALU.add)
                        g.op("dve", "tensor_add",
                             sbf[:, dc * 1024:(dc + 1) * 1024],
                             [tmpf[:], xres[:]],
                             out=sbf[:, dc * 1024:(dc + 1) * 1024],
                             in0=tmpf[:], in1=xres[:])

                    def emit(cc, sl):
                        bsl = lnb[:]
                        g.op("act", "copy", bsl, [sl], out=bsl, in_=sl)
                        g.dma("sp", sA[b, cc * 128:(cc + 1) * 128,
                                       t0:t0 + 1024], sl)
                        g.dma("act", sAb[b, cc * 128:(cc + 1) * 128,
                                         t0:t0 + 1024], bsl)

                    build_layernorm(g, bufs, sbf[:], psS2_, psQ2_, psB1_,
                                    psH, emit)
    g.barrier()


def _stage_p3(nc, g, es, sb, psum, dram, win, cin, gl, eyeb):
    sA, sAb, sB, sBb = dram["sA"], dram["sAb"], dram["sB"], dram["sBb"]
    onescol, ones_k1, epscol = gl["onescol"], gl["ones_k1"], gl["epscol"]
    for b in range(NB):
        with _scoped(g) as st:
            qraw = sb(st, "p3q", (128, NCC * T), BF16)
            kraw = sb(st, "p3k", (128, 2 * T), BF16)
            vraw = sb(st, "p3v", (128, 2 * T), BF16)
            yb = sb(st, "p3yb", (128, NCC * T), BF16)
            with _scoped(g) as s2:
                xb = sb(s2, "p3xb", (128, NCC * T), BF16)
                wq = sb(s2, "p3wq", (128, NCC * DM), BF16)
                wk = sb(s2, "p3wk", (128, NCC * 256), BF16)
                wv = sb(s2, "p3wv", (128, NCC * 256), BF16)
                cosq = sb(s2, "p3cq", (128, T), BF16)
                sinq = sb(s2, "p3sq", (128, T), F32)
                cosk = sb(s2, "p3ck", (128, T), BF16)
                sink = sb(s2, "p3sk", (128, T), F32)
                perm = sb(s2, "p3pm", (128, 128), BF16)
                t1b = sb(s2, "p3t1", (128, 512), BF16)
                t2b = sb(s2, "p3t2", (128, 512), BF16)
                psP = psum(s2, "p3psP", (128, 512), F32)
                psW = psum(s2, "p3psW", (128, 512), F32)
                stg3 = sb(s2, "p3stg", (128, 2048), F32)

                def lc3(dst, srcv, blocks, bw):
                    gsz = max(1, 2048 // bw)
                    for q in range(0, blocks, gsz):
                        nb = min(gsz, blocks - q)
                        w_ = nb * bw
                        g.dma("sp",
                              stg3[:, 0:w_].rearrange("p (n w) -> p n w",
                                                      n=nb),
                              srcv[:, q:q + nb, :])
                        g.op("dve", "tensor_copy",
                             dst[:, q * bw:q * bw + w_],
                             [stg3[:]], out=dst[:, q * bw:q * bw + w_],
                             in_=stg3[:, 0:w_])

                lc3(wq[:], win["L1_attn_Wq"].rearrange(
                    "(cc p) d -> p cc d", p=128), NCC, DM)
                lc3(wk[:], win["L1_attn_Wk"].rearrange(
                    "(cc p) d -> p cc d", p=128), NCC, 256)
                lc3(wv[:], win["L1_attn_Wv"].rearrange(
                    "(cc p) d -> p cc d", p=128), NCC, 256)
                g.dma("sp", cosq[:], cin["c_cosq"])
                g.dma("sp", sinq[:], cin["c_sinq"])
                g.dma("sp", cosk[:], cin["c_cos"])
                g.dma("sp", sink[:], cin["c_sin"])
                g.dma("sp", perm[:], cin["c_perm"])
                for cc in range(NCC):
                    g.dma("sp", xb[:, cc * T:(cc + 1) * T],
                          sAb[b, cc * 128:(cc + 1) * 128, :])
                for dst, w_, nw, cosx, sinx in [
                        (qraw, wq, NCC, cosq, sinq),
                        (kraw, wk, 2, cosk, sink), (vraw, wv, 2, None, None)]:
                    for gc in range(nw):
                        for tq in range(4):
                            g.mm(psP[:],
                                 [(w_[:, cc * (nw * 128) + gc * 128:
                                      cc * (nw * 128) + (gc + 1) * 128],
                                   xb[:, cc * T + tq * 512:
                                      cc * T + (tq + 1) * 512])
                                  for cc in range(NCC)])
                            sl = dst[:, gc * T + tq * 512:
                                     gc * T + (tq + 1) * 512]
                            g.op("dve", "tensor_copy", sl, [psP[:]],
                                 out=sl, in_=psP[:])
                            if cosx is not None:
                                g.mm(psW[:], [(perm[:], sl)])
                                g.op("dve", "tensor_tensor", t1b[:],
                                     [psW[:], sinx[:]], out=t1b[:],
                                     in0=psW[:],
                                     in1=sinx[:, tq * 512:(tq + 1) * 512],
                                     op=ALU.mult)
                                g.op("dve", "tensor_tensor", t2b[:],
                                     [sl, cosx[:]], out=t2b[:], in0=sl,
                                     in1=cosx[:, tq * 512:(tq + 1) * 512],
                                     op=ALU.mult)
                                g.op("dve", "tensor_add", sl,
                                     [t1b[:], t2b[:]], out=sl, in0=t1b[:],
                                     in1=t2b[:])
            with _scoped(g) as s2:
                kh = sb(s2, "p3kh", (64, T), BF16)
                vh = sb(s2, "p3vh", (64, T), BF16)
                vT = sb(s2, "p3vT", (128, 16 * 64), BF16)
                qh = sb(s2, "p3qh", (64, T), BF16)
                maskT4 = sb(s2, "p3mt4", (128, 2048), F32)
                onebc = sb(s2, "p3onebc", (128, 1), BF16)
                probT2 = [sb(s2, f"p3pT{i}", (128, 16 * 512), BF16)
                          for i in range(2)]
                tmpd = sb(s2, "p3tmpd", (128, 512), F32)
                sqb = sb(s2, "p3sqb", (64, T), F32)
                nrow = sb(s2, "p3nrow", (1, T), F32)
                nk = sb(s2, "p3nk", (1, 1), F32)
                nq = sb(s2, "p3nq", (1, 1), F32)
                mcol = sb(s2, "p3mcol", (128, 1), F32)
                denr = [sb(s2, f"p3dr{i}", (1, 512), F32) for i in range(2)]
                rb = sb(s2, "p3rb", (64, 512), F32)
                psS = psum(s2, "p3psS", (128, 512), F32)
                psS2 = psum(s2, "p3psS2", (128, 512), F32)
                psY2 = [psum(s2, f"p3psY{i}", (64, 512), F32)
                        for i in range(2)]
                psDen = [psum(s2, f"p3psD{i}", (1, 512), F32)
                         for i in range(2)]
                psVT = psum(s2, "p3psVT", (128, 64), BF16)
                psB = psum(s2, "p3psB", (128, 512), F32)
                g.dma("sp", maskT4[:], cin["c_maskT4"])
                g.op("dve", "memset", onebc[:], [], ap=onebc[:],
                     constant=1.0)

                def colnorm_max(src64, out11):
                    g.op("act", "activation", sqb[:], [src64], out=sqb[:],
                         in_=src64, func=AF.Square)
                    for sgm in range(4):
                        g.mm(psB[0:1, 0:512],
                             [(onescol[0:64, :],
                               sqb[:, sgm * 512:(sgm + 1) * 512])])
                        g.op("dve", "tensor_copy",
                             nrow[:, sgm * 512:(sgm + 1) * 512], [psB[:]],
                             out=nrow[:, sgm * 512:(sgm + 1) * 512],
                             in_=psB[0:1, 0:512])
                    g.op("dve", "reduce_max", out11, [nrow[:]], out=out11,
                         in_=nrow[:], axis=AX.X, negate=False)
                    g.op("act", "activation", out11, [out11], out=out11,
                         in_=out11, func=AF.Sqrt)

                for j in range(4):
                    base = (j % 2) * 64
                    g.op("dve", "tensor_copy", kh[:],
                         [kraw[:]], out=kh[:],
                         in_=kraw[base:base + 64, (j // 2) * T:
                                  (j // 2) * T + T])
                    g.op("dve", "tensor_copy", vh[:],
                         [vraw[:]], out=vh[:],
                         in_=vraw[base:base + 64, (j // 2) * T:
                                  (j // 2) * T + T])
                    for c in range(16):
                        g.tr(psVT[:], vh[:, c * 128:(c + 1) * 128],
                             eyeb[0:64, 0:64])
                        g.op("dve", "tensor_copy", vT[:, c * 64:(c + 1) * 64],
                             [psVT[:]], out=vT[:, c * 64:(c + 1) * 64],
                             in_=psVT[:])
                    colnorm_max(kh[:], nk[:])
                    for h in range(4 * j, 4 * j + 4):
                        qbase = (h % 2) * 64
                        g.op("dve", "tensor_copy", qh[:], [qraw[:]],
                             out=qh[:],
                             in_=qraw[qbase:qbase + 64,
                                      (h // 2) * T:(h // 2) * T + T])
                        colnorm_max(qh[:], nq[:])
                        g.op("dve", "tensor_mul", nq[:], [nq[:], nk[:]],
                             out=nq[:], in0=nq[:], in1=nk[:])
                        g.op("dve", "tensor_scalar", nq[:], [nq[:]],
                             out=nq[:], in0=nq[:], scalar1=-1.0,
                             scalar2=-1.0, op0=ALU.mult, op1=ALU.add)
                        g.mm(psB[:, 0:1], [(ones_k1[:], nq[:])])
                        g.op("dve", "tensor_copy", mcol[:], [psB[:]],
                             out=mcol[:], in_=psB[:, 0:1])

                        def p3_ST(tq):
                            qp = tq % 2
                            pT = probT2[qp]
                            nkc = (tq + 1) * 4
                            for c in range(nkc):
                                ps = psS if c % 2 == 0 else psS2
                                g.mm(ps[:],
                                     [(kh[:, c * 128:(c + 1) * 128],
                                       qh[:, tq * 512:(tq + 1) * 512])])
                                m = c - tq * 4
                                if m >= 0:
                                    g.op("dve", "tensor_add", tmpd[:],
                                         [ps[:], maskT4[:]], out=tmpd[:],
                                         in0=ps[:],
                                         in1=maskT4[:, m * 512:
                                                    (m + 1) * 512])
                                    g.op("act", "activation",
                                         pT[:, c * 512:(c + 1) * 512],
                                         [tmpd[:], mcol[:]],
                                         out=pT[:, c * 512:(c + 1) * 512],
                                         in_=tmpd[:], func=AF.Exp,
                                         bias=mcol[:])
                                else:
                                    g.op("act", "activation",
                                         pT[:, c * 512:(c + 1) * 512],
                                         [ps[:], mcol[:]],
                                         out=pT[:, c * 512:(c + 1) * 512],
                                         in_=ps[:], func=AF.Exp,
                                         bias=mcol[:])

                        def p3_denav(tq):
                            qp = tq % 2
                            pT = probT2[qp]
                            psY = psY2[qp]
                            nkc = (tq + 1) * 4
                            items = []
                            for c in range(nkc):
                                items.append(
                                    (psDen[qp][:], onebc[:],
                                     pT[:, c * 512:(c + 1) * 512],
                                     c == 0, c == nkc - 1))
                                items.append(
                                    (psY[:], vT[:, c * 64:(c + 1) * 64],
                                     pT[:, c * 512:(c + 1) * 512],
                                     c == 0, c == nkc - 1))
                            g.mm_multi(items,
                                       outs=[psDen[qp][:], psY[:]],
                                       ins=[onebc[:], vT[:], pT[:]])
                            g.op("dve", "tensor_copy", denr[qp][:],
                                 [psDen[qp][:]], out=denr[qp][:],
                                 in_=psDen[qp][:])
                            g.op("dve", "tensor_scalar_max", denr[qp][:],
                                 [denr[qp][:]], out=denr[qp][:],
                                 in0=denr[qp][:], scalar1=1e-38)
                            g.op("dve", "reciprocal", denr[qp][:],
                                 [denr[qp][:]], out=denr[qp][:],
                                 in_=denr[qp][:])
                            g.mm(psB[0:64, :], [(ones_k1[:, 0:64],
                                                 denr[qp][:])])
                            g.op("act", "copy", rb[:], [psB[:]], out=rb[:],
                                 in_=psB[0:64, :])
                            dsl = yb[qbase:qbase + 64,
                                     (h // 2) * T + tq * 512:
                                     (h // 2) * T + (tq + 1) * 512]
                            g.op("dve", "tensor_tensor", dsl,
                                 [psY[:], rb[:]], out=dsl, in0=psY[:],
                                 in1=rb[:], op=ALU.mult)

                        p3_ST(0)
                        for tq in range(1, 4):
                            p3_ST(tq)
                            p3_denav(tq - 1)
                        p3_denav(3)
            with _scoped(g) as s3:
                wo = sb(s3, "p3wo", (128, NCC * DM), BF16)
                stg4 = sb(s3, "p3stg4", (128, 2048), F32)
                wov = win["L1_attn_Wo"].rearrange(
                    "(cc p) d -> p cc d", p=128)
                for q in range(0, NCC, 2):
                    g.dma("sp",
                          stg4[:].rearrange("p (n w) -> p n w", n=2),
                          wov[:, q:q + 2, :])
                    g.op("dve", "tensor_copy",
                         wo[:, q * DM:(q + 2) * DM],
                         [stg4[:]], out=wo[:, q * DM:(q + 2) * DM],
                         in_=stg4[:])
                sbf = sb(s3, "p3s", (128, NCC * 1024), F32)
                xres = sb(s3, "p3xres", (128, 1024), F32)
                tmpf = sb(s3, "p3tmpf", (128, 1024), F32)
                psO = psum(s3, "p3psO", (128, 1024), F32)
                psQ2 = psum(s3, "p3psQ2", (128, 1024), F32)
                psB1 = psum(s3, "p3psB1", (128, 1024), F32)
                psB2 = psum(s3, "p3psB2", (128, 1024), F32)
                bufs = {"onescol": onescol, "ones_k1": ones_k1,
                        "epscol": epscol,
                        "sq": sb(s3, "p3sq2", (128, 1024), F32),
                        "meanr": sb(s3, "p3mr", (1, 1024), F32),
                        "m2r": sb(s3, "p3m2", (1, 1024), F32),
                        "varr": sb(s3, "p3vr", (1, 1024), F32),
                        "lnf": sb(s3, "p3lnf", (128, 1024), F32)}
                lnb = sb(s3, "p3lnb", (128, 1024), BF16)
                for hf in range(2):
                    t0 = hf * 1024
                    for dc in range(NCC):
                        for th in range(2):
                            g.mm(psO[:, th * 512:(th + 1) * 512],
                                 [(wo[:, cc * DM + dc * 128:
                                      cc * DM + (dc + 1) * 128],
                                   yb[:, cc * T + t0 + th * 512:
                                      cc * T + t0 + th * 512 + 512])
                                  for cc in range(NCC)])
                        g.dma("sp", xres[:],
                              sA[b, dc * 128:(dc + 1) * 128,
                                 t0:t0 + 1024])
                        g.op("dve", "tensor_add",
                             sbf[:, dc * 1024:(dc + 1) * 1024],
                             [psO[:], xres[:]],
                             out=sbf[:, dc * 1024:(dc + 1) * 1024],
                             in0=psO[:], in1=xres[:])

                    def emit(cc, sl, t0=t0):
                        bsl = lnb[:]
                        g.op("act", "copy", bsl, [sl], out=bsl, in_=sl)
                        g.dma("sp", sB[b, cc * 128:(cc + 1) * 128,
                                       t0:t0 + 1024], sl)
                        g.dma("act", sBb[b, cc * 128:(cc + 1) * 128,
                                         t0:t0 + 1024], bsl)

                    build_layernorm(g, bufs, sbf[:], psO, psQ2, psB1,
                                    psB2, emit)
    g.barrier()


def _stage_p4(nc, g, es, sb, psum, dram, win, cin, gl, eyeb, eyef):
    sB, sBb, y = dram["sB"], dram["sBb"], dram["y"]
    onescol, ones_k1, epscol = gl["onescol"], gl["ones_k1"], gl["epscol"]
    with _scoped(g) as st:
        rwT = sb(st, "p4rw", (128, NCC * 7), F32)
        oh7 = sb(st, "p4oh", (7, 896), BF16)
        g.dma("sp", oh7[:], cin["c_oh7"])
        with _scoped(g) as sr:
            rtmp = sb(sr, "p4rt", (7, DM), F32)
            psR7 = psum(sr, "p4psR7", (128, 128), F32)
            g.dma("sp", rtmp[:], win["L1_ffn_router_w"])
            for cc in range(NCC):
                g.tr(psR7[:, 0:7], rtmp[:, cc * 128:(cc + 1) * 128],
                     eyef[0:7, 0:7])
                g.op("dve", "tensor_copy", rwT[:, cc * 7:(cc + 1) * 7],
                     [psR7[:]], out=rwT[:, cc * 7:(cc + 1) * 7],
                     in_=psR7[:, 0:7])

        for tt in range(4):
            b, t0 = tt // 2, (tt % 2) * 1024
            with _scoped(g) as s2:
                xb = sb(s2, "p4xb", (128, NCC * 1024), BF16)
                hb = sb(s2, "p4hb", (128, 32 * 1024), BF16)
                acc = sb(s2, "p4acc", (128, NCC * 1024), F32)
                cT = sb(s2, "p4cT", (7, 1024), BF16)
                Bt = sb(s2, "p4Bt", (128, 1024), BF16)
                tmpf = sb(s2, "p4tmpf", (128, 1024), F32)
                lg = sb(s2, "p4lg", (128, 7), F32)
                sm1 = sb(s2, "p4sm1", (128, 1), F32)
                sm2 = sb(s2, "p4sm2", (128, 1), F32)
                w1c = sb(s2, "p4w1c", (128, 1), F32)
                i1 = sb(s2, "p4i1", (128, 7), F32)
                i2 = sb(s2, "p4i2", (128, 7), F32)
                mk = sb(s2, "p4mk", (128, 7), F32)
                cwb = sb(s2, "p4cwb", (128, 7), BF16)

                ser = ExitStack()
                xf32r = sb(ser, "p4xf32r", (128, 1024), F32)
                lgall4 = sb(ser, "p4lgall", (128, 56), F32)
                psL = psum(ser, "p4psL", (128, 16), F32)
                psCT = psum(ser, "p4psCT", (16, 128), BF16)
                for cc in range(NCC):
                    g.dma("sp", xb[:, cc * 1024:(cc + 1) * 1024],
                          sBb[b, cc * 128:(cc + 1) * 128, t0:t0 + 1024])
                for cc in range(NCC):
                    g.dma("sp", xf32r[:],
                          sB[b, cc * 128:(cc + 1) * 128, t0:t0 + 1024])
                    for i in range(8):
                        g.mm(psL[:, 0:7],
                             [(xf32r[:, i * 128:(i + 1) * 128],
                               rwT[:, cc * 7:(cc + 1) * 7])])
                        if cc == 0:
                            g.op("dve", "tensor_copy",
                                 lgall4[:, i * 7:(i + 1) * 7], [psL[:]],
                                 out=lgall4[:, i * 7:(i + 1) * 7],
                                 in_=psL[:, 0:7])
                        else:
                            g.op("dve", "tensor_add",
                                 lgall4[:, i * 7:(i + 1) * 7],
                                 [psL[:], lgall4[:]],
                                 out=lgall4[:, i * 7:(i + 1) * 7],
                                 in0=lgall4[:, i * 7:(i + 1) * 7],
                                 in1=psL[:, 0:7])
                for i in range(8):
                    g.op("dve", "tensor_copy", lg[:], [lgall4[:]],
                         out=lg[:], in_=lgall4[:, i * 7:(i + 1) * 7])
                    g.op("dve", "reduce_max", sm1[:], [lg[:]], out=sm1[:],
                         in_=lg[:], axis=AX.X, negate=True)
                    g.op("dve", "tensor_scalar", i1[:], [lg[:], sm1[:]],
                         out=i1[:], in0=lg[:], scalar1=sm1[:], scalar2=None,
                         op0=ALU.add)
                    g.op("act", "activation", i1[:], [i1[:]], out=i1[:],
                         in_=i1[:], func=AF.Sign)
                    g.op("dve", "tensor_scalar_add", i1[:], [i1[:]],
                         out=i1[:], in0=i1[:], scalar1=1.0)
                    g.op("dve", "scalar_tensor_tensor", mk[:], [i1[:], lg[:]],
                         out=mk[:], in0=i1[:], scalar=-1e30, in1=lg[:],
                         op0=ALU.mult, op1=ALU.add)
                    g.op("dve", "reduce_max", sm2[:], [mk[:]], out=sm2[:],
                         in_=mk[:], axis=AX.X, negate=True)
                    g.op("dve", "tensor_scalar", i2[:], [mk[:], sm2[:]],
                         out=i2[:], in0=mk[:], scalar1=sm2[:], scalar2=None,
                         op0=ALU.add)
                    g.op("act", "activation", i2[:], [i2[:]], out=i2[:],
                         in_=i2[:], func=AF.Sign)
                    g.op("dve", "tensor_scalar_add", i2[:], [i2[:]],
                         out=i2[:], in0=i2[:], scalar1=1.0)
                    g.op("dve", "tensor_sub", w1c[:], [sm2[:], sm1[:]],
                         out=w1c[:], in0=sm2[:], in1=sm1[:])
                    g.op("act", "activation", w1c[:], [w1c[:]], out=w1c[:],
                         in_=w1c[:], func=AF.Sigmoid)
                    g.op("dve", "tensor_scalar_mul", i1[:], [i1[:], w1c[:]],
                         out=i1[:], in0=i1[:], scalar1=w1c[:])
                    g.op("dve", "tensor_scalar", w1c[:], [w1c[:]],
                         out=w1c[:], in0=w1c[:], scalar1=-1.0, scalar2=1.0,
                         op0=ALU.mult, op1=ALU.add)
                    g.op("dve", "tensor_scalar_mul", i2[:], [i2[:], w1c[:]],
                         out=i2[:], in0=i2[:], scalar1=w1c[:])
                    g.op("dve", "tensor_add", cwb[:], [i1[:], i2[:]],
                         out=cwb[:], in0=i1[:], in1=i2[:])
                    g.tr(psCT[0:7, :], cwb[:], eyeb[:])
                    g.op("dve", "tensor_copy", cT[:, i * 128:(i + 1) * 128],
                         [psCT[:]], out=cT[:, i * 128:(i + 1) * 128],
                         in_=psCT[0:7, :])
                g.barrier()
                ser.close()
                psG2 = [psum(s2, f"p4psG{i}", (128, 1024), F32)
                        for i in range(2)]
                psU2 = [psum(s2, f"p4psU{i}", (128, 1024), F32)
                        for i in range(2)]
                sg2 = [sb(s2, f"p4sg2{i}", (128, 1024), BF16)
                       for i in range(2)]
                sw_ = ExitStack()
                wg_b = [sb(sw_, f"p4wg{i}", (128, 4096), BF16)
                        for i in range(2)]
                wu_b = [sb(sw_, f"p4wu{i}", (128, 4096), BF16)
                        for i in range(2)]
                wd_b = [sb(sw_, f"p4wd{i}", (128, 4096), BF16)
                        for i in range(2)]
                for ei, e in enumerate([7] + list(range(7))):
                    if e == 7:
                        gim, uim = dram["sg_s"], dram["su_s"]
                        dim_ = dram["sd_s"]
                    else:
                        gim, uim = dram["gate_s"][e], dram["up_s"][e]
                        dim_ = dram["down_s"][e]
                    for hs in range(8):
                        pp = hs % 2
                        g.dma("sp", wg_b[pp][:], gim[hs])
                        g.dma("act", wu_b[pp][:], uim[hs])
                        for hl in range(4):
                            hc = hs * 4 + hl
                            hp = hl % 2
                            items = []
                            for th in range(2):
                                for cc in range(NCC):
                                    items.append(
                                        (psG2[hp][:, th * 512:
                                                   (th + 1) * 512],
                                         wg_b[pp][:, cc * 512 + hl * 128:
                                                  cc * 512 +
                                                  (hl + 1) * 128],
                                         xb[:, cc * 1024 + th * 512:
                                            cc * 1024 + th * 512 + 512],
                                         cc == 0, cc == NCC - 1))
                                for cc in range(NCC):
                                    items.append(
                                        (psU2[hp][:, th * 512:
                                                   (th + 1) * 512],
                                         wu_b[pp][:, cc * 512 + hl * 128:
                                                  cc * 512 +
                                                  (hl + 1) * 128],
                                         xb[:, cc * 1024 + th * 512:
                                            cc * 1024 + th * 512 + 512],
                                         cc == 0, cc == NCC - 1))
                            g.mm_multi(items,
                                       outs=[psG2[hp][:], psU2[hp][:]],
                                       ins=[wg_b[pp][:], wu_b[pp][:],
                                            xb[:]])
                            g.op("act", "activation", sg2[hp][:],
                                 [psG2[hp][:]], out=sg2[hp][:],
                                 in_=psG2[hp][:], func=AF.Silu)
                            g.op("dve", "tensor_tensor",
                                 hb[:, hc * 1024:(hc + 1) * 1024],
                                 [psU2[hp][:], sg2[hp][:]],
                                 out=hb[:, hc * 1024:(hc + 1) * 1024],
                                 in0=psU2[hp][:], in1=sg2[hp][:],
                                 op=ALU.mult)
                    if e != 7:
                        for th in range(2):
                            g.mm(psU2[1][:, th * 512:(th + 1) * 512],
                                 [(oh7[:, e * 128:(e + 1) * 128],
                                   cT[:, th * 512:(th + 1) * 512])])
                        g.op("act", "copy", Bt[:], [psU2[1][:]], out=Bt[:],
                             in_=psU2[1][:])
                    psD4 = [psG2[0], psG2[1], psU2[0], psU2[1]]
                    for p_ in range(2):
                        for hr in range(4):
                            g.dma("sp", wd_b[hr % 2][:], dim_[p_, hr])
                            items = []
                            for hcc in range(8):
                                for dc in range(4):
                                    for th in range(2):
                                        items.append(
                                            (psD4[dc][:, th * 512:
                                                      (th + 1) * 512],
                                             wd_b[hr % 2][:,
                                                          hcc * 512 +
                                                          dc * 128:
                                                          hcc * 512 +
                                                          (dc + 1) * 128],
                                             hb[:,
                                                (hr * 8 + hcc) * 1024 +
                                                th * 512:
                                                (hr * 8 + hcc) * 1024 +
                                                th * 512 + 512],
                                             hr == 0 and hcc == 0,
                                             hr == 3 and hcc == 7))
                            g.mm_multi(items, outs=[p[:] for p in psD4],
                                       ins=[wd_b[hr % 2][:], hb[:]])
                        for dc in range(4):
                            d_ = p_ * 4 + dc
                            asl = acc[:, d_ * 1024:(d_ + 1) * 1024]
                            if e == 7:
                                g.op("dve", "tensor_copy", asl,
                                     [psD4[dc][:]], out=asl,
                                     in_=psD4[dc][:])
                            else:
                                g.op("dve", "tensor_tensor",
                                     tmpf[:], [psD4[dc][:], Bt[:]],
                                     out=tmpf[:], in0=psD4[dc][:],
                                     in1=Bt[:], op=ALU.mult)
                                g.op("pool", "tensor_add", asl,
                                     [acc[:], tmpf[:]], out=asl, in0=asl,
                                     in1=tmpf[:])
                g.barrier()
                sw_.close()
                with _scoped(g) as s3:
                    xres = sb(s3, "p4xres", (128, 1024), F32)
                    obuf = sb(s3, "p4ob", (128, 2 * 1024), F32)
                    lnf2 = sb(s3, "p4lnf2", (128, NCC * 1024), F32)
                    bufs = {"onescol": onescol, "ones_k1": ones_k1,
                            "epscol": epscol,
                            "sq": sb(s3, "p4sq", (128, 1024), F32),
                            "meanr": sb(s3, "p4mr", (1, 1024), F32),
                            "m2r": sb(s3, "p4m2", (1, 1024), F32),
                            "varr": sb(s3, "p4vr", (1, 1024), F32),
                            "lnf": sb(s3, "p4lnf", (128, 1024), F32)}
                    for dc in range(NCC):
                        g.dma("sp", xres[:],
                              sB[b, dc * 128:(dc + 1) * 128, t0:t0 + 1024])
                        g.op("dve", "tensor_add",
                             acc[:, dc * 1024:(dc + 1) * 1024],
                             [acc[:], xres[:]],
                             out=acc[:, dc * 1024:(dc + 1) * 1024],
                             in0=acc[:, dc * 1024:(dc + 1) * 1024],
                             in1=xres[:])

                    def emit(cc, sl):
                        g.op("pool", "tensor_copy",
                             lnf2[:, cc * 1024:(cc + 1) * 1024], [sl],
                             out=lnf2[:, cc * 1024:(cc + 1) * 1024], in_=sl)

                    build_layernorm(g, bufs, acc[:], psU2[0], psU2[1],
                                    psG2[0], psG2[1], emit)
                    for t8 in range(8):
                        ob = obuf[:, (t8 % 2) * 1024:(t8 % 2 + 1) * 1024]
                        for cc in range(NCC):
                            ps = psG2[0] if cc % 2 == 0 else psG2[1]
                            g.tr(ps[:, 0:128],
                                 lnf2[:, cc * 1024 + t8 * 128:
                                      cc * 1024 + (t8 + 1) * 128], eyef[:])
                            g.op("dve", "tensor_copy",
                                 ob[:, cc * 128:(cc + 1) * 128], [ps[:]],
                                 out=ob[:, cc * 128:(cc + 1) * 128],
                                 in_=ps[:, 0:128])
                        g.dma("sp", y[b, t0 + t8 * 128:t0 + (t8 + 1) * 128,
                                      :], ob)
    g.barrier()


_BUILD_CACHE = {}


def _prep_weight(name, v):
    vv = np.ascontiguousarray(v, dtype=np.float32)
    if name in ("L0_attn_time_mix_k", "L0_attn_time_mix_v",
                "L0_attn_time_mix_r"):
        vv = vv.reshape(-1)
    return vv


def kernel(**inputs):
    stages = tuple(DEV_STAGES)
    key = (stages, DEV_TAPS)
    if key not in _BUILD_CACHE:
        _BUILD_CACHE[key] = build_program(stages, DEV_TAPS)
    nc, cst, declared = _BUILD_CACHE[key]

    x = np.ascontiguousarray(inputs["L0_attn_x"])
    base = {}
    for name, v in inputs.items():
        if name != "L0_attn_x" and name in declared:
            base[name] = _prep_weight(name, v)
    for name, v in cst.items():
        if name in declared:
            base[name] = v
    in_maps = []
    for core in range(8):
        m = dict(base)
        if "p0" in stages:
            m["xin"] = x[2 * core:2 * core + 2]
        in_maps.append(m)
    res = run_bass_kernel_spmd(nc, in_maps, core_ids=list(range(8)))
    out = np.concatenate([np.asarray(r["y"]) for r in res.results], axis=0)
    return out.astype(np.float32)
